# version 06
# baseline (speedup 1.0000x reference)
import numpy as np
import ml_dtypes
from contextlib import ExitStack
import concourse.bass as bass
import concourse.mybir as mybir
from concourse.bass import AP
from concourse.bass_utils import run_bass_kernel_spmd
from concourse.aot_env import allow_ici_all_to_all

F32 = mybir.dt.float32
BF16 = mybir.dt.bfloat16
AF = mybir.ActivationFunctionType
ALU = mybir.AluOpType
AX = mybir.AxisListType

B_, L_, D = 4, 8192, 2048
TC = 128
NCH = 32
ND = 16
DIN = 96
NST = 16
DTR = 6
EPS = 1e-5
HALF = 4096

ENGINES = ("sync", "vector", "scalar", "tensor", "gpsimd")


class Buf:
    __slots__ = ("w", "r")
    def __init__(self):
        self.w = None
        self.r = []


class KB:
    ENG_SEM = {"tensor": "pe", "vector": "dve", "scalar": "act",
               "gpsimd": "pool", "sync": "snc"}
    SEMS = ("pe", "dve", "act", "pool", "snc", "din", "dout", "cc")

    def __init__(self):
        self.ops = {e: [] for e in ENGINES}
        self.cnt = {s: 0 for s in self.SEMS}

    def op(self, eng, fn, r=(), w=(), s=(), dma=False, cc=False):
        waits = {}

        def need(ev, hard):
            if ev is None:
                return
            sm, v = ev
            if (not hard) and (not dma) and (not cc) \
                    and sm == self.ENG_SEM[eng] and sm == "pe":
                return
            if waits.get(sm, 0) < v:
                waits[sm] = v

        for b in r:
            need(b.w, False)
        for b in s:
            need(b.w, True)
        for b in w:
            need(b.w, True)
            for ev in b.r:
                need(ev, False)
        if cc:
            sem, d = "cc", 1
        elif dma:
            sem, d = ("din" if eng == "sync" else "dout"), 16
        else:
            sem, d = self.ENG_SEM[eng], 1
        self.cnt[sem] += d
        ev = (sem, self.cnt[sem])
        self.ops[eng].append((fn, sorted(waits.items()), sem, d))
        for b in tuple(r) + tuple(s):
            b.r.append(ev)
        for b in w:
            b.w = ev
            b.r = []
        return ev


def build_program():
    nc = bass.Bass("TRN2", target_bir_lowering=False)
    kb = KB()

    def dram_in(name, shape, dtype=F32):
        return nc.dram_tensor(name, shape, dtype, kind="ExternalInput")

    t_xloc = dram_in("xloc", [4224, D])
    t_rx = dram_in("rx", [1024, D])
    t_sw = dram_in("swsl", [8, 524288])
    t_swb = dram_in("swb", [1, 8])
    t_w1 = dram_in("w1", [8, 128, 128])
    t_b1 = dram_in("b1", [8, 128])
    t_w2 = dram_in("w2", [8, 128, 128])
    t_b2 = dram_in("b2", [8, 128])
    t_w3 = dram_in("w3", [D, D])
    t_b3 = dram_in("b3", [1, D])
    t_lnw = dram_in("lnw", [1, D])
    t_lnb = dram_in("lnb", [1, D])
    t_ipW = dram_in("ipW", [DIN, D])
    t_convw = dram_in("convw", [DIN, 3])
    t_convb = dram_in("convb", [DIN, 1])
    t_xpW = dram_in("xpW", [38, DIN])
    t_dtW = dram_in("dtW", [DIN, DTR])
    t_dtb = dram_in("dtb", [DIN, 1])
    t_Alog = dram_in("Alog", [DIN, NST])
    t_Dp = dram_in("Dp", [DIN, 1])
    t_opW = dram_in("opW", [D, DIN])
    t_ident = dram_in("ident", [128, 128], BF16)
    t_bsel = dram_in("bsel", [4, 1])
    t_psel = dram_in("psel", [8, 1])
    t_ml = dram_in("ml", [DIN, 1])
    t_mr = dram_in("mr", [DIN, 1])

    t_yout = nc.dram_tensor("yout", [HALF, D], F32, kind="ExternalOutput")

    t_x2p = nc.dram_tensor("x2p", [HALF, D], F32)
    t_x2b = nc.dram_tensor("x2b", [2048, D], F32)
    t_a2a = nc.dram_tensor("a2a", [2048, D], F32)
    t_dsc = nc.dram_tensor("dsc", [DIN, HALF], F32)
    t_csc = nc.dram_tensor("csc", [NST, HALF], F32)
    t_l1i = nc.dram_tensor("l1i", [4, 8], F32)
    t_l1o = nc.dram_tensor("l1o", [4, 8], F32)
    t_sti = nc.dram_tensor("sti", [1, DIN * NST], F32)
    t_sto = nc.dram_tensor("sto", [8, DIN * NST], F32)
    t_l3i = nc.dram_tensor("l3i", [4, 8], F32)
    t_l3o = nc.dram_tensor("l3o", [4, 8], F32)
    t_tiny = nc.dram_tensor("tiny", [1, 64], F32)
    import os
    KDBG = bool(os.environ.get("KDBG"))
    t_dbg = nc.dram_tensor("dbg", [128, 16384], F32, kind="ExternalOutput") \
        if KDBG else None
    dbg_reg = {}
    dbg_off = [0]
    t_xirow = nc.dram_tensor("xirow", [1, DIN * NST], F32)

    dbuf = {t: Buf() for t in ("xloc", "rx", "sw", "yout", "x2p", "x2b", "a2a",
                               "dsc", "csc", "l1i", "l1o", "sti", "sto",
                               "l3i", "l3o", "tiny", "xirow")}

    with ExitStack() as stack:
        en = stack.enter_context
        S = {}
        BF = {}

        def sb(name, shape, dtype=F32):
            S[name] = en(nc.sbuf_tensor("s_" + name, shape, dtype))
            BF[name] = Buf()

        def psum(name, shape, dtype=F32):
            S[name] = en(nc.psum_tensor("p_" + name, shape, dtype))
            BF[name] = Buf()

        sb("w3T", [128, ND * D], BF16)
        sb("opWT", [DIN, D], BF16)
        sb("ipWT", [128, ND * DIN], BF16)
        sb("xpWT", [DIN, DIN], BF16)
        sb("dtWT", [DTR, DIN], BF16)
        sb("ident", [128, 128], BF16)
        sb("onesb", [1, 512], BF16)
        sb("onesc", [128, 1], F32)
        sb("b3p", [1, D], BF16)
        sb("App", [DIN, NST], F32)
        sb("Arep", [DIN, NST * TC], F32)
        sb("Dp", [DIN, 1], F32)
        sb("convw", [DIN, 3], F32)
        sb("convb", [DIN, 1], F32)
        sb("dtb", [DIN, 1], F32)
        sb("ipb", [DIN, 1], F32)
        sb("lnw", [1, D], F32)
        sb("lnbcols", [128, ND], F32)
        sb("lnbb", [128, ND], BF16)
        sb("swb", [1, 8], F32)
        sb("bsel", [4, 1], F32)
        sb("psel", [8, 1], F32)
        sb("ml", [DIN, 1], F32)
        sb("mr", [DIN, 1], F32)
        sb("b1s", [8, 128], F32)
        sb("b2s", [8, 128], F32)
        sb("w1all", [128, 8 * 128], F32)
        sb("w2all", [128, 8 * 128], F32)
        sb("w1T0", [128, 128], BF16)
        sb("w1T1", [128, 128], BF16)
        sb("w2T0", [128, 128], BF16)
        sb("w2T1", [128, 128], BF16)
        sb("b1r0", [1, 128], BF16)
        sb("b1r1", [1, 128], BF16)
        sb("b2r4", [1, 512], BF16)
        sb("ws0", [128, 128], F32)
        sb("ws1", [128, 128], F32)
        sb("wsb", [128, 128], BF16)
        sb("mrow", [128, 8], F32)
        sb("xbuf", [128, D], F32)
        sb("z1f", [128, D], F32)
        sb("x1b", [128, D], F32)
        sb("zscr", [128, D], F32)
        sb("bft", [128, D], BF16)
        sb("z1T", [128, D], BF16)
        sb("zoT", [128, D], BF16)
        sb("hb0", [128, 512], BF16)
        sb("hb1", [128, 512], BF16)
        sb("sc1", [DIN, NST * TC], F32)
        sb("sc2", [DIN, NST * TC], F32)
        sb("sc3", [DIN, NST * TC], F32)
        sb("racc", [128, 8], F32)
        sb("racc2", [128, 8], F32)
        sb("cvx0", [DIN, 130], F32)
        sb("cvx1", [DIN, 130], F32)
        sb("cvx2", [DIN, 130], F32)
        sb("cvxH", [DIN, 130], F32)
        sb("cvA", [DIN, TC], F32)
        sb("cvB", [DIN, TC], F32)
        sb("ubuf", [DIN, TC], F32)
        sb("ub16", [DIN, TC], BF16)
        sb("dbc", [DIN, TC], F32)
        sb("dr6", [DTR, TC], BF16)
        sb("sca", [DIN, TC], F32)
        sb("scb", [DIN, TC], F32)
        sb("spb", [DIN, TC], F32)
        sb("Gb", [DIN, TC], F32)
        sb("ybuf", [DIN, TC], F32)
        sb("Cc", [NST, TC], F32)
        sb("t16", [DIN, NST], F32)
        sb("state", [DIN, NST], F32)
        sb("xinit", [DIN, NST], F32)
        sb("ystore", [DIN, HALF], F32)
        sb("ybf", [DIN, TC], BF16)
        sb("s1c", [128, 1], F32)
        sb("s2c", [128, 1], F32)
        sb("mean", [128, 1], F32)
        sb("var", [128, 1], F32)
        sb("rstd", [128, 1], F32)
        sb("nmr", [128, 1], F32)
        sb("rstd2", [128, 1], F32)
        sb("scomb", [128, 1], F32)
        sb("nms", [128, 1], F32)
        sb("tiny1", [128, 1], F32)
        sb("ceps", [128, 1], F32)
        sb("cone", [128, 1], F32)
        sb("lg", [4, 8], F32)
        sb("swbb", [4, 8], F32)
        sb("lrow", [1, 8], F32)
        sb("nmx", [1, 1], F32)
        sb("ssum", [1, 1], F32)
        sb("rsum", [1, 1], F32)
        sb("probs", [1, 8], F32)
        sb("m1", [1, 8], F32)
        sb("m2", [1, 8], F32)
        sb("p2", [1, 8], F32)
        sb("mx1", [1, 1], F32)
        sb("mx2", [1, 1], F32)
        sb("g1", [1, 1], F32)
        sb("g2", [1, 1], F32)
        sb("j8", [1, 8], F32)
        sb("coef", [1, 8], F32)
        sb("coefb", [1, 8], F32)
        sb("m1T", [8, 1], F32)
        sb("m2T", [8, 1], F32)
        sb("coefT", [8, 1], F32)
        sb("sgrow", [8, 4], F32)

        psum("psA", [128, 512], F32)
        psum("psB", [128, 512], F32)
        psum("psW0", [128, 512], F32)
        psum("psW1", [128, 512], F32)
        psum("psS", [128, 512], F32)
        psum("psQ", [128, 512], F32)
        psum("psT", [128, 1024], BF16)
        psum("psT2", [128, 1024], BF16)

        S["rscr"] = S["bft"]; BF["rscr"] = BF["bft"]
        S["zob"] = S["bft"]; BF["zob"] = BF["bft"]
        S["rsw"] = S["x1b"]; BF["rsw"] = BF["x1b"]
        S["rsw2"] = S["z1f"]; BF["rsw2"] = BF["z1f"]
        S["rxt"] = S["xbuf"]; BF["rxt"] = BF["xbuf"]
        S["rprod"] = S["xbuf"]; BF["rprod"] = BF["xbuf"]
        S["rpart"] = S["tiny1"]; BF["rpart"] = BF["tiny1"]

        sems = {}
        for smn in KB.SEMS:
            sems[smn] = en(nc.semaphore(smn))
        block = en(nc.Block())

        def E(eng, fn, r=(), w=(), s=(), dma=False, cc=False):
            return kb.op(eng, fn, r=r, w=w, s=s, dma=dma, cc=cc)

        dbgbuf = Buf()

        def dbg(name, ap, rbufs):
            if not KDBG:
                return
            p, f = ap.shape[-2], ap.shape[-1]
            off = dbg_off[0]
            dbg_reg[name] = (off, p, f)
            dbg_off[0] += f
            E("gpsimd", lambda x, ap=ap, off=off, p=p, f=f: x.dma_start(
                out=t_dbg.ap()[0:p, off:off + f], in_=ap,
                allow_slow_non_contiguous=True),
              r=list(rbufs), w=[dbgbuf], dma=True)
        build_program.dbg_reg = dbg_reg

        def ld(dst_ap, src_ap, rbufs, wbufs, eng="sync", slow=False):
            if slow:
                E(eng, lambda x, d=dst_ap, so=src_ap: x.dma_start(
                    out=d, in_=so, allow_slow_non_contiguous=True),
                  r=rbufs, w=wbufs, dma=True)
            else:
                E(eng, lambda x, d=dst_ap, so=src_ap: x.dma_start(out=d, in_=so),
                  r=rbufs, w=wbufs, dma=True)

        def bcast(dst_ap, src_row_ap, nrep, rbufs, wbufs, eng="gpsimd"):
            f = src_row_ap.shape[-1]
            src = AP(src_row_ap.tensor, src_row_ap.offset, [[1, 1], [0, nrep], [1, f]])
            E(eng, lambda x, d=dst_ap, so=src: x.dma_start(out=d, in_=so),
              r=rbufs, w=wbufs, dma=True)

        def repn(dst_ap, src_2d_ap, p, rep, f, rbufs, wbufs, eng="gpsimd"):
            src = AP(src_2d_ap.tensor, src_2d_ap.offset, [[1, p], [0, rep], [1, f]])
            E(eng, lambda x, d=dst_ap, so=src: x.dma_start(out=d, in_=so),
              r=rbufs, w=wbufs, dma=True)

        def transpose16(srcname, dstname):
            for j in range(ND):
                pst = "psT" if (j % 16) < 8 else "psT2"
                slot = (j % 8) * 128
                E("tensor", lambda x, j=j, pst=pst, slot=slot: x.transpose(
                    out=S[pst][:, slot:slot + 128],
                    in_=S[srcname][:, j * 128:(j + 1) * 128],
                    identity=S["ident"][:, :]),
                  r=[BF[srcname], BF["ident"]], w=[BF[pst]])
                if j % 8 == 7:
                    pst2 = "psT" if j < 8 else "psT2"
                    j0 = j - 7
                    E("vector", lambda x, j0=j0, pst2=pst2, dstname=dstname:
                      x.tensor_copy(out=S[dstname][:, j0 * 128:(j0 + 8) * 128],
                                    in_=S[pst2][:, :]),
                      r=[BF[pst2]], w=[BF[dstname]])

        ld(S["ident"][:, :], t_ident.ap(), [], [BF["ident"]])
        ld(S["b1s"][:, :], t_b1.ap(), [], [BF["b1s"]])
        ld(S["b2s"][:, :], t_b2.ap(), [], [BF["b2s"]])
        ld(S["swb"][:, :], t_swb.ap(), [], [BF["swb"]])
        ld(S["bsel"][:, :], t_bsel.ap(), [], [BF["bsel"]])
        ld(S["psel"][:, :], t_psel.ap(), [], [BF["psel"]])
        ld(S["ml"][:, :], t_ml.ap(), [], [BF["ml"]])
        ld(S["mr"][:, :], t_mr.ap(), [], [BF["mr"]])
        ld(S["convw"][:, :], t_convw.ap(), [], [BF["convw"]])
        ld(S["convb"][:, :], t_convb.ap(), [], [BF["convb"]])
        ld(S["dtb"][:, :], t_dtb.ap(), [], [BF["dtb"]])
        ld(S["Dp"][:, :], t_Dp.ap(), [], [BF["Dp"]])
        ld(S["lnw"][:, :], t_lnw.ap(), [], [BF["lnw"]])
        ld(S["lnbcols"][:, :],
           t_lnb.ap().rearrange("a (k p) -> (a p) k", p=128), [], [BF["lnbcols"]],
           slow=True)
        for ee in range(8):
            ld(S["w1all"][:, ee * 128:(ee + 1) * 128], t_w1.ap()[ee], [], [BF["w1all"]])
            ld(S["w2all"][:, ee * 128:(ee + 1) * 128], t_w2.ap()[ee], [], [BF["w2all"]])
        E("gpsimd", lambda x: x.memset(S["onesb"][:, :], 1.0), w=[BF["onesb"]])
        E("gpsimd", lambda x: x.memset(S["onesc"][:, :], 1.0), w=[BF["onesc"]])
        E("gpsimd", lambda x: x.memset(S["ceps"][:, :], EPS), w=[BF["ceps"]])
        E("gpsimd", lambda x: x.memset(S["cone"][:, :], 1.0), w=[BF["cone"]])
        ld(S["App"][:, :], t_Alog.ap(), [], [BF["App"]])
        E("scalar", lambda x: x.activation(out=S["App"][:, :], in_=S["App"][:, :],
                                           func=AF.Exp),
          s=[BF["App"]], w=[BF["App"]])
        E("vector", lambda x: x.tensor_scalar_mul(out=S["App"][:, :],
                                                  in0=S["App"][:, :], scalar1=-0.001),
          r=[BF["App"]], w=[BF["App"]])
        E("gpsimd", lambda x: x.memset(S["cvA"][:, :], 1.0), w=[BF["cvA"]])
        for n in range(NST):
            E("vector", lambda x, n=n: x.tensor_scalar_mul(
                out=S["Arep"][:, n * TC:(n + 1) * TC], in0=S["cvA"][:, :],
                scalar1=S["App"][:, n:n + 1]),
              r=[BF["cvA"]], s=[BF["App"]], w=[BF["Arep"]])
        ld(S["zscr"][0:1, :], t_b3.ap(), [], [BF["zscr"]])
        E("scalar", lambda x: x.activation(out=S["b3p"][:, :], in_=S["zscr"][0:1, :],
                                           func=AF.Copy, scale=0.5),
          r=[BF["zscr"]], w=[BF["b3p"]])

        for j in range(ND):
            ld(S["zscr"][:, :], t_w3.ap()[j * 128:(j + 1) * 128, :], [], [BF["zscr"]])
            E("scalar", lambda x: x.activation(
                out=S["bft"][:, :], in_=S["zscr"][:, :], func=AF.Copy, scale=0.5),
              r=[BF["zscr"]], w=[BF["bft"]])
            for k in range(ND):
                pst = "psT" if k < 8 else "psT2"
                slot = (k % 8) * 128
                E("tensor", lambda x, k=k, pst=pst, slot=slot: x.transpose(
                    out=S[pst][:, slot:slot + 128],
                    in_=S["bft"][:, k * 128:(k + 1) * 128],
                    identity=S["ident"][:, :]),
                  r=[BF["bft"], BF["ident"]], w=[BF[pst]])
                if k % 8 == 7:
                    pst2 = "psT" if k < 8 else "psT2"
                    k0 = k - 7
                    w3v = S["w3T"][:, :].rearrange("p (k c) -> p k c", c=D)
                    E("vector", lambda x, j=j, k0=k0, pst2=pst2, w3v=w3v: x.tensor_copy(
                        out=w3v[:, k0:k0 + 8, j * 128:(j + 1) * 128],
                        in_=S[pst2][:, :].rearrange("p (k c) -> p k c", c=128)),
                      r=[BF[pst2]], w=[BF["w3T"]])

        for j in range(ND):
            ld(S["zscr"][:, 0:DIN], t_opW.ap()[j * 128:(j + 1) * 128, :],
               [], [BF["zscr"]])
            E("scalar", lambda x: x.activation(
                out=S["bft"][:, 0:DIN], in_=S["zscr"][:, 0:DIN], func=AF.Copy),
              r=[BF["zscr"]], w=[BF["bft"]])
            pst = "psT" if (j % 2) == 0 else "psT2"
            E("tensor", lambda x, pst=pst: x.transpose(
                out=S[pst][0:DIN, 0:128], in_=S["bft"][:, 0:DIN],
                identity=S["ident"][:, :]),
              r=[BF["bft"], BF["ident"]], w=[BF[pst]])
            E("vector", lambda x, j=j, pst=pst: x.tensor_copy(
                out=S["opWT"][:, j * 128:(j + 1) * 128], in_=S[pst][0:DIN, 0:128]),
              r=[BF[pst]], w=[BF["opWT"]])

        ld(S["zscr"][0:DIN, :], t_ipW.ap(), [], [BF["zscr"]])
        E("scalar", lambda x: x.activation(
            out=S["bft"][0:DIN, :], in_=S["zscr"][0:DIN, :], func=AF.Copy),
          r=[BF["zscr"]], w=[BF["bft"]])
        for j in range(ND):
            pst = "psT" if (j % 2) == 0 else "psT2"
            E("tensor", lambda x, j=j, pst=pst: x.transpose(
                out=S[pst][:, 0:DIN], in_=S["bft"][0:DIN, j * 128:(j + 1) * 128],
                identity=S["ident"][0:DIN, 0:DIN]),
              r=[BF["bft"], BF["ident"]], w=[BF[pst]])
            E("vector", lambda x, j=j, pst=pst: x.tensor_copy(
                out=S["ipWT"][:, j * DIN:(j + 1) * DIN], in_=S[pst][:, 0:DIN]),
              r=[BF[pst]], w=[BF["ipWT"]])
        E("scalar", lambda x: x.activation(out=S["lnbb"][:, :],
                                           in_=S["lnbcols"][:, :], func=AF.Copy),
          r=[BF["lnbcols"]], w=[BF["lnbb"]])
        for j in range(ND):
            E("tensor", lambda x, j=j: x.matmul(
                out=S["psS"][0:DIN, 256:257],
                lhsT=S["ipWT"][:, j * DIN:(j + 1) * DIN],
                rhs=S["lnbb"][:, j:j + 1],
                start=(j == 0), stop=(j == ND - 1)),
              r=[BF["ipWT"], BF["lnbb"]], w=[BF["psS"]])
        E("vector", lambda x: x.tensor_copy(out=S["ipb"][:, :],
                                            in_=S["psS"][0:DIN, 256:257]),
          r=[BF["psS"]], w=[BF["ipb"]])
        bcast(S["x1b"][0:DIN, :], S["lnw"][0:1, :], DIN, [BF["lnw"]], [BF["x1b"]])
        E("vector", lambda x: x.tensor_mul(out=S["zscr"][0:DIN, :],
                                           in0=S["zscr"][0:DIN, :],
                                           in1=S["x1b"][0:DIN, :]),
          r=[BF["zscr"], BF["x1b"]], w=[BF["zscr"]])
        E("scalar", lambda x: x.activation(
            out=S["bft"][0:DIN, :], in_=S["zscr"][0:DIN, :], func=AF.Copy),
          r=[BF["zscr"]], w=[BF["bft"]])
        for j in range(ND):
            pst = "psT" if (j % 2) == 0 else "psT2"
            E("tensor", lambda x, j=j, pst=pst: x.transpose(
                out=S[pst][:, 0:DIN], in_=S["bft"][0:DIN, j * 128:(j + 1) * 128],
                identity=S["ident"][0:DIN, 0:DIN]),
              r=[BF["bft"], BF["ident"]], w=[BF[pst]])
            E("vector", lambda x, j=j, pst=pst: x.tensor_copy(
                out=S["ipWT"][:, j * DIN:(j + 1) * DIN], in_=S[pst][:, 0:DIN]),
              r=[BF[pst]], w=[BF["ipWT"]])

        E("gpsimd", lambda x: x.memset(S["xpWT"][:, :], 0.0), w=[BF["xpWT"]])
        for (r0, nr, dst0, scale) in ((0, DTR, 0, 1.0), (DTR, NST, 32, 0.001),
                                      (DTR + NST, NST, 64, 1.0)):
            ld(S["zscr"][0:nr, 0:DIN], t_xpW.ap()[r0:r0 + nr, :], [], [BF["zscr"]])
            E("scalar", lambda x, nr=nr, scale=scale: x.activation(
                out=S["bft"][0:nr, 0:DIN], in_=S["zscr"][0:nr, 0:DIN],
                func=AF.Copy, scale=scale),
              r=[BF["zscr"]], w=[BF["bft"]])
            E("tensor", lambda x, nr=nr: x.transpose(
                out=S["psT"][0:DIN, 0:nr], in_=S["bft"][0:nr, 0:DIN],
                identity=S["ident"][0:nr, 0:nr]),
              r=[BF["bft"], BF["ident"]], w=[BF["psT"]])
            E("vector", lambda x, nr=nr, dst0=dst0: x.tensor_copy(
                out=S["xpWT"][:, dst0:dst0 + nr], in_=S["psT"][0:DIN, 0:nr]),
              r=[BF["psT"]], w=[BF["xpWT"]])

        ld(S["zscr"][0:DIN, 0:DTR], t_dtW.ap(), [], [BF["zscr"]])
        E("scalar", lambda x: x.activation(out=S["bft"][0:DIN, 0:DTR],
                                           in_=S["zscr"][0:DIN, 0:DTR], func=AF.Copy),
          r=[BF["zscr"]], w=[BF["bft"]])
        E("tensor", lambda x: x.transpose(out=S["psT"][0:DTR, 0:DIN],
                                          in_=S["bft"][0:DIN, 0:DTR],
                                          identity=S["ident"][0:DIN, 0:DIN]),
          r=[BF["bft"], BF["ident"]], w=[BF["psT"]])
        E("vector", lambda x: x.tensor_copy(out=S["dtWT"][:, :],
                                            in_=S["psT"][0:DTR, 0:DIN]),
          r=[BF["psT"]], w=[BF["dtWT"]])

        def ln_stats(src_name):
            E("scalar", lambda x: x.activation(out=S["zscr"][:, :],
                                               in_=S[src_name][:, :], func=AF.Copy,
                                               accum_out=S["s1c"][:, 0:1]),
              r=[BF[src_name]], w=[BF["zscr"], BF["s1c"]])
            E("scalar", lambda x: x.activation(out=S["zscr"][:, :],
                                               in_=S[src_name][:, :], func=AF.Square,
                                               accum_out=S["s2c"][:, 0:1]),
              r=[BF[src_name]], w=[BF["zscr"], BF["s2c"]])
            E("vector", lambda x: x.tensor_scalar_mul(out=S["mean"][:, :],
                                                      in0=S["s1c"][:, :],
                                                      scalar1=1.0 / D),
              r=[BF["s1c"]], w=[BF["mean"]])
            E("scalar", lambda x: x.activation(out=S["tiny1"][:, :],
                                               in_=S["mean"][:, :], func=AF.Square),
              s=[BF["mean"]], w=[BF["tiny1"]])
            E("vector", lambda x: x.tensor_scalar_mul(out=S["var"][:, :],
                                                      in0=S["s2c"][:, :],
                                                      scalar1=1.0 / D),
              r=[BF["s2c"]], w=[BF["var"]])
            E("vector", lambda x: x.tensor_sub(out=S["var"][:, :], in0=S["var"][:, :],
                                               in1=S["tiny1"][:, :]),
              r=[BF["var"], BF["tiny1"]], w=[BF["var"]])
            E("scalar", lambda x: x.activation(out=S["rstd"][:, :], in_=S["var"][:, :],
                                               func=AF.Ln, bias=S["ceps"][:, 0:1]),
              s=[BF["var"], BF["ceps"]], w=[BF["rstd"]])
            E("scalar", lambda x: x.activation(out=S["rstd"][:, :], in_=S["rstd"][:, :],
                                               func=AF.Exp, scale=-0.5),
              s=[BF["rstd"]], w=[BF["rstd"]])
            E("vector", lambda x: x.scalar_tensor_tensor(
                out=S["nmr"][:, :], in0=S["mean"][:, :], scalar=-1.0,
                in1=S["rstd"][:, :], op0=ALU.mult, op1=ALU.mult),
              r=[BF["mean"]], s=[BF["rstd"]], w=[BF["nmr"]])

        def ffn_body(zof, xsrc, xdst):
            for g in range(4):
                for k in range(2):
                    pA = "psA" if k == 0 else "psB"
                    w1T = "w1T0" if k == 0 else "w1T1"
                    b1r = "b1r0" if k == 0 else "b1r1"
                    E("tensor", lambda x, pA=pA, b1r=b1r: x.matmul(
                        out=S[pA][:, :], lhsT=S[b1r][:, :], rhs=S["onesb"][:, :],
                        start=True, stop=False),
                      r=[BF[b1r], BF["onesb"]], w=[BF[pA]])
                    for jj in range(4):
                        j = 4 * g + jj
                        E("tensor", lambda x, pA=pA, w1T=w1T, j=j, jj=jj: x.matmul(
                            out=S[pA][:, jj * 128:(jj + 1) * 128],
                            lhsT=S[w1T][:, :],
                            rhs=S["z1T"][:, j * 128:(j + 1) * 128],
                            start=False, stop=(jj == 3)),
                          r=[BF[w1T], BF["z1T"]], w=[BF[pA]])
                    hb = "hb0" if k == 0 else "hb1"
                    E("scalar", lambda x, pA=pA, hb=hb: x.activation(
                        out=S[hb][:, :], in_=S[pA][:, :], func=AF.Gelu),
                      r=[BF[pA]], w=[BF[hb]])
                E("tensor", lambda x: x.matmul(
                    out=S["psA"][:, :], lhsT=S["onesb"][:, 0:128],
                    rhs=S["b2r4"][:, :], start=True, stop=False),
                  r=[BF["onesb"], BF["b2r4"]], w=[BF["psA"]])
                for k in range(2):
                    hb = "hb0" if k == 0 else "hb1"
                    w2T = "w2T0" if k == 0 else "w2T1"
                    for jj in range(4):
                        E("tensor", lambda x, hb=hb, w2T=w2T, jj=jj, k=k: x.matmul(
                            out=S["psA"][:, jj * 128:(jj + 1) * 128],
                            lhsT=S[hb][:, jj * 128:(jj + 1) * 128],
                            rhs=S[w2T][:, :],
                            start=False, stop=(k == 1 and jj == 3)),
                          r=[BF[hb], BF[w2T]], w=[BF["psA"]])
                E("vector", lambda x, g=g, zof=zof: x.tensor_add(
                    out=S["zob"][:, g * 512:(g + 1) * 512],
                    in0=S["psA"][:, :], in1=S[zof][:, g * 512:(g + 1) * 512]),
                  r=[BF["psA"], BF[zof]], w=[BF["zob"]])
            transpose16("zob", "zoT")
            for fc in range(4):
                pw = "psW0" if fc % 2 == 0 else "psW1"
                E("tensor", lambda x, fc=fc, pw=pw: x.matmul(
                    out=S[pw][:, :], lhsT=S["onesb"][:, 0:128],
                    rhs=S["b3p"][:, fc * 512:(fc + 1) * 512],
                    start=True, stop=False),
                  r=[BF["onesb"], BF["b3p"]], w=[BF[pw]])
                for j in range(ND):
                    E("tensor", lambda x, fc=fc, j=j, pw=pw: x.matmul(
                        out=S[pw][:, :],
                        lhsT=S["zoT"][:, j * 128:(j + 1) * 128],
                        rhs=S["w3T"][:, j * D + fc * 512: j * D + (fc + 1) * 512],
                        start=False, stop=(j == ND - 1)),
                      r=[BF["zoT"], BF["w3T"]], w=[BF[pw]])
                E("vector", lambda x, fc=fc, pw=pw, xsrc=xsrc, xdst=xdst: x.tensor_add(
                    out=S[xdst][:, fc * 512:(fc + 1) * 512],
                    in0=S[pw][:, :], in1=S[xsrc][:, fc * 512:(fc + 1) * 512]),
                  r=[BF[pw], BF[xsrc]], w=[BF[xdst]])

        def router(src_rows_fn, srcbuf, l_in_t, l_in_buf, l_out_t, l_out_buf):
            for b in range(4):
                for i in range(2):
                    ld(S["rxt"][:, :], src_rows_fn(b, i), [srcbuf], [BF["rxt"]])
                    ln_stats("rxt")
                    E("scalar", lambda x: x.activation(
                        out=S["zscr"][:, :], in_=S["rxt"][:, :], func=AF.Identity,
                        bias=S["nmr"][:, 0:1], scale=S["rstd"][:, 0:1]),
                      r=[BF["rxt"]], s=[BF["nmr"], BF["rstd"]], w=[BF["zscr"]])
                    for ee in range(8):
                        swsl = "rsw" if ee % 2 == 0 else "rsw2"
                        src = t_sw.ap()[ee:ee + 1,
                                        i * 262144:(i + 1) * 262144]
                        src = src.rearrange("a (p f) -> (a p) f", p=128)
                        ld(S[swsl][:, :], src, [dbuf["sw"]], [BF[swsl]])
                        E("vector", lambda x, swsl=swsl: x.tensor_mul(
                            out=S["rprod"][:, :], in0=S["zscr"][:, :],
                            in1=S[swsl][:, :]),
                          r=[BF["zscr"], BF[swsl]], w=[BF["rprod"]])
                        if i == 0:
                            E("vector", lambda x, ee=ee: x.reduce_sum(
                                out=S["racc"][:, ee:ee + 1], in_=S["rprod"][:, :],
                                axis=AX.X),
                              r=[BF["rprod"]], w=[BF["racc"]])
                        else:
                            E("vector", lambda x: x.reduce_sum(
                                out=S["rpart"][:, :], in_=S["rprod"][:, :],
                                axis=AX.X),
                              r=[BF["rprod"]], w=[BF["rpart"]])
                            E("vector", lambda x, ee=ee: x.tensor_add(
                                out=S["racc2"][:, ee:ee + 1],
                                in0=S["racc"][:, ee:ee + 1], in1=S["rpart"][:, :]),
                              r=[BF["racc"], BF["rpart"]], w=[BF["racc2"]])
                E("tensor", lambda x, b=b: x.matmul(
                    out=S["psQ"][0:8, 256 + b:256 + b + 1], lhsT=S["racc2"][:, :],
                    rhs=S["onesc"][:, :], start=True, stop=True),
                  r=[BF["racc2"], BF["onesc"]], w=[BF["psQ"]])
            E("vector", lambda x: x.tensor_copy(out=S["sgrow"][:, :],
                                                in_=S["psQ"][0:8, 256:260]),
              r=[BF["psQ"]], w=[BF["sgrow"]])
            E("gpsimd", lambda x, t=l_in_t: x.dma_start(
                out=t.ap().rearrange("b e -> e b"), in_=S["sgrow"][:, :],
                allow_slow_non_contiguous=True),
              r=[BF["sgrow"]], w=[l_in_buf], dma=True)
            E("gpsimd", lambda x, ti=l_in_t, to=l_out_t: x.collective_compute(
                "AllReduce", ALU.add, replica_groups=[list(range(8))],
                ins=[ti.ap()], outs=[to.ap()]),
              r=[l_in_buf], w=[l_out_buf], cc=True)
            ld(S["lg"][:, :], l_out_t.ap(), [l_out_buf], [BF["lg"]])
            bcast(S["swbb"][:, :], S["swb"][0:1, :], 4, [BF["swb"]], [BF["swbb"]])
            E("vector", lambda x: x.tensor_add(out=S["lg"][:, :], in0=S["lg"][:, :],
                                               in1=S["swbb"][:, :]),
              r=[BF["lg"], BF["swbb"]], w=[BF["lg"]])

        def selection():
            E("tensor", lambda x: x.matmul(out=S["psQ"][0:1, 0:8],
                                           lhsT=S["bsel"][:, :], rhs=S["lg"][:, :],
                                           start=True, stop=True),
              r=[BF["bsel"], BF["lg"]], w=[BF["psQ"]])
            E("vector", lambda x: x.tensor_copy(out=S["lrow"][:, :],
                                                in_=S["psQ"][0:1, 0:8]),
              r=[BF["psQ"]], w=[BF["lrow"]])
            E("vector", lambda x: x.reduce_max(out=S["nmx"][:, :],
                                               in_=S["lrow"][:, :], axis=AX.X),
              r=[BF["lrow"]], w=[BF["nmx"]])
            E("vector", lambda x: x.tensor_scalar_mul(out=S["nmx"][:, :],
                                                      in0=S["nmx"][:, :],
                                                      scalar1=-1.0),
              r=[BF["nmx"]], w=[BF["nmx"]])
            E("vector", lambda x: x.tensor_scalar_add(out=S["j8"][:, :],
                                                      in0=S["lrow"][:, :],
                                                      scalar1=S["nmx"][0:1, 0:1]),
              r=[BF["lrow"]], s=[BF["nmx"]], w=[BF["j8"]])
            E("vector", lambda x: x.tensor_scalar_max(out=S["j8"][:, :],
                                                      in0=S["j8"][:, :],
                                                      scalar1=-80.0),
              r=[BF["j8"]], w=[BF["j8"]])
            E("scalar", lambda x: x.activation(out=S["probs"][:, :],
                                               in_=S["j8"][:, :], func=AF.Exp),
              r=[BF["j8"]], w=[BF["probs"]])
            E("vector", lambda x: x.reduce_sum(out=S["ssum"][:, :],
                                               in_=S["probs"][:, :], axis=AX.X),
              r=[BF["probs"]], w=[BF["ssum"]])
            E("vector", lambda x: x.reciprocal(out=S["rsum"][:, :],
                                               in_=S["ssum"][:, :]),
              r=[BF["ssum"]], w=[BF["rsum"]])
            E("vector", lambda x: x.tensor_scalar_mul(out=S["probs"][:, :],
                                                      in0=S["probs"][:, :],
                                                      scalar1=S["rsum"][0:1, 0:1]),
              r=[BF["probs"]], s=[BF["rsum"]], w=[BF["probs"]])
            E("vector", lambda x: x.reduce_max(out=S["mx1"][:, :],
                                               in_=S["probs"][:, :], axis=AX.X),
              r=[BF["probs"]], w=[BF["mx1"]])
            E("vector", lambda x: x.tensor_single_scalar(
                out=S["m1"][:, :], in_=S["probs"][:, :],
                scalar=S["mx1"][0:1, 0:1], op=ALU.is_ge),
              r=[BF["probs"]], s=[BF["mx1"]], w=[BF["m1"]])
            E("vector", lambda x: x.scalar_tensor_tensor(
                out=S["p2"][:, :], in0=S["m1"][:, :], scalar=-1000.0,
                in1=S["probs"][:, :], op0=ALU.mult, op1=ALU.add),
              r=[BF["m1"], BF["probs"]], w=[BF["p2"]])
            E("vector", lambda x: x.reduce_max(out=S["mx2"][:, :],
                                               in_=S["p2"][:, :], axis=AX.X),
              r=[BF["p2"]], w=[BF["mx2"]])
            E("vector", lambda x: x.tensor_single_scalar(
                out=S["m2"][:, :], in_=S["p2"][:, :],
                scalar=S["mx2"][0:1, 0:1], op=ALU.is_ge),
              r=[BF["p2"]], s=[BF["mx2"]], w=[BF["m2"]])
            E("vector", lambda x: x.tensor_mul(out=S["j8"][:, :],
                                               in0=S["probs"][:, :],
                                               in1=S["m1"][:, :]),
              r=[BF["probs"], BF["m1"]], w=[BF["j8"]])
            E("vector", lambda x: x.reduce_sum(out=S["g1"][:, :], in_=S["j8"][:, :],
                                               axis=AX.X),
              r=[BF["j8"]], w=[BF["g1"]])
            E("vector", lambda x: x.tensor_mul(out=S["j8"][:, :],
                                               in0=S["probs"][:, :],
                                               in1=S["m2"][:, :]),
              r=[BF["probs"], BF["m2"]], w=[BF["j8"]])
            E("vector", lambda x: x.reduce_sum(out=S["g2"][:, :], in_=S["j8"][:, :],
                                               axis=AX.X),
              r=[BF["j8"]], w=[BF["g2"]])
            E("vector", lambda x: x.tensor_scalar_mul(out=S["coef"][:, :],
                                                      in0=S["m1"][:, :],
                                                      scalar1=S["g1"][0:1, 0:1]),
              r=[BF["m1"]], s=[BF["g1"]], w=[BF["coef"]])
            E("vector", lambda x: x.tensor_scalar_mul(out=S["coefb"][:, :],
                                                      in0=S["m2"][:, :],
                                                      scalar1=S["g2"][0:1, 0:1]),
              r=[BF["m2"]], s=[BF["g2"]], w=[BF["coefb"]])
            E("vector", lambda x: x.tensor_add(out=S["coef"][:, :],
                                               in0=S["coef"][:, :],
                                               in1=S["coefb"][:, :]),
              r=[BF["coef"], BF["coefb"]], w=[BF["coef"]])
            for nm, dst in (("m1", "m1T"), ("m2", "m2T"), ("coef", "coefT")):
                E("gpsimd", lambda x, nm=nm: x.dma_start(
                    out=t_tiny.ap()[0:1, 0:8], in_=S[nm][:, :]),
                  r=[BF[nm]], w=[dbuf["tiny"]], dma=True)
                ld(S[dst][:, :],
                   t_tiny.ap()[0:1, 0:8].rearrange("a (p f) -> (a p) f", p=8),
                   [dbuf["tiny"]], [BF[dst]], slow=True)
            for k, (mT, b1r) in enumerate((("m1T", "b1r0"), ("m2T", "b1r1"))):
                E("tensor", lambda x, mT=mT, k=k: x.matmul(
                    out=S["psQ"][0:1, 128 * k:128 * k + 128], lhsT=S[mT][:, :],
                    rhs=S["b1s"][:, :], start=True, stop=True),
                  r=[BF[mT], BF["b1s"]], w=[BF["psQ"]])
                E("scalar", lambda x, b1r=b1r, k=k: x.activation(
                    out=S[b1r][:, :], in_=S["psQ"][0:1, 128 * k:128 * k + 128],
                    func=AF.Copy),
                  r=[BF["psQ"]], w=[BF[b1r]])
            E("tensor", lambda x: x.matmul(
                out=S["psQ"][0:1, 256:384], lhsT=S["coefT"][:, :],
                rhs=S["b2s"][:, :], start=True, stop=True),
              r=[BF["coefT"], BF["b2s"]], w=[BF["psQ"]])
            E("scalar", lambda x: x.activation(
                out=S["b2r4"][:, 0:128], in_=S["psQ"][0:1, 256:384], func=AF.Copy),
              r=[BF["psQ"]], w=[BF["b2r4"]])
            b2v = S["b2r4"][0:1, 0:128]
            E("gpsimd", lambda x: x.dma_start(
                out=S["b2r4"][0:1, 128:512],
                in_=AP(b2v.tensor, b2v.offset, [[1, 1], [0, 3], [1, 128]])),
              r=[BF["b2r4"]], w=[BF["b2r4"]], dma=True)
            for (mname, wall, wT, scl) in (
                    ("m1", "w1all", "w1T0", None), ("m2", "w1all", "w1T1", None),
                    ("m1", "w2all", "w2T0", "g1"), ("m2", "w2all", "w2T1", "g2")):
                if scl is None:
                    bcast(S["mrow"][:, :], S[mname][0:1, :], 128,
                          [BF[mname]], [BF["mrow"]])
                else:
                    E("vector", lambda x, mname=mname, scl=scl: x.tensor_scalar_mul(
                        out=S["j8"][:, :], in0=S[mname][:, :],
                        scalar1=S[scl][0:1, 0:1]),
                      r=[BF[mname]], s=[BF[scl]], w=[BF["j8"]])
                    bcast(S["mrow"][:, :], S["j8"][0:1, :], 128,
                          [BF["j8"]], [BF["mrow"]])
                E("vector", lambda x, wall=wall: x.tensor_scalar_mul(
                    out=S["ws0"][:, :], in0=S[wall][:, 0:128],
                    scalar1=S["mrow"][:, 0:1]),
                  r=[BF[wall]], s=[BF["mrow"]], w=[BF["ws0"]])
                for ee in range(1, 8):
                    src_b = "ws0" if ee % 2 == 1 else "ws1"
                    dst_b = "ws1" if ee % 2 == 1 else "ws0"
                    E("vector", lambda x, wall=wall, ee=ee, src_b=src_b, dst_b=dst_b:
                      x.scalar_tensor_tensor(
                        out=S[dst_b][:, :], in0=S[wall][:, ee * 128:(ee + 1) * 128],
                        scalar=S["mrow"][:, ee:ee + 1], in1=S[src_b][:, :],
                        op0=ALU.mult, op1=ALU.add),
                      r=[BF[wall], BF[src_b]], s=[BF["mrow"]], w=[BF[dst_b]])
                E("scalar", lambda x: x.activation(out=S["wsb"][:, :],
                                                   in_=S["ws1"][:, :], func=AF.Copy),
                  r=[BF["ws1"]], w=[BF["wsb"]])
                E("tensor", lambda x: x.transpose(
                    out=S["psT"][:, 0:128], in_=S["wsb"][:, :],
                    identity=S["ident"][:, :]),
                  r=[BF["wsb"], BF["ident"]], w=[BF["psT"]])
                E("vector", lambda x, wT=wT: x.tensor_copy(out=S[wT][:, :],
                                                           in_=S["psT"][:, 0:128]),
                  r=[BF["psT"]], w=[BF[wT]])

        def rx_rows(b, i):
            return t_rx.ap()[b * 256 + i * 128: b * 256 + (i + 1) * 128, :]
        router(rx_rows, dbuf["rx"], t_l1i, dbuf["l1i"], t_l1o, dbuf["l1o"])
        selection()
        dbg("lg", S["lg"][:, :], [BF["lg"]])
        dbg("racc2", S["racc2"][:, :], [BF["racc2"]])
        dbg("sgrow", S["sgrow"][:, :], [BF["sgrow"]])
        dbg("lrow", S["lrow"][:, :], [BF["lrow"]])
        dbg("probs", S["probs"][:, :], [BF["probs"]])
        dbg("m1", S["m1"][:, :], [BF["m1"]])
        dbg("m2", S["m2"][:, :], [BF["m2"]])
        dbg("g1", S["g1"][:, :], [BF["g1"]])
        dbg("g2", S["g2"][:, :], [BF["g2"]])
        dbg("ws1", S["ws1"][:, :], [BF["ws1"]])
        dbg("ipb", S["ipb"][:, :], [BF["ipb"]])
        dbg("App", S["App"][:, :], [BF["App"]])
        dbg("Arep", S["Arep"][:, 0:256], [BF["Arep"]])

        E("gpsimd", lambda x: x.memset(S["state"][:, :], 0.0), w=[BF["state"]])

        def mk_z1T(srcname):
            transpose16(srcname, "z1T")

        def pass1_chunk(c, halo):
            row0 = 4096 if halo else c * TC
            ld(S["xbuf"][:, :], t_xloc.ap()[row0:row0 + TC, :],
               [dbuf["xloc"]], [BF["xbuf"]])
            ln_stats("xbuf")
            E("scalar", lambda x: x.activation(out=S["z1f"][:, :], in_=S["xbuf"][:, :],
                                               func=AF.Identity, bias=S["nmr"][:, 0:1],
                                               scale=S["rstd"][:, 0:1]),
              r=[BF["xbuf"]], s=[BF["nmr"], BF["rstd"]], w=[BF["z1f"]])
            E("scalar", lambda x: x.activation(out=S["bft"][:, :], in_=S["z1f"][:, :],
                                               func=AF.Copy),
              r=[BF["z1f"]], w=[BF["bft"]])
            mk_z1T("bft")
            ffn_body("z1f", "xbuf", "x1b")
            ln_stats("x1b")
            E("scalar", lambda x: x.activation(out=S["rstd2"][:, :],
                                               in_=S["rstd"][:, :], func=AF.Square),
              s=[BF["rstd"]], w=[BF["rstd2"]])
            E("vector", lambda x: x.tensor_mul(out=S["rstd2"][:, :],
                                               in0=S["var"][:, :],
                                               in1=S["rstd2"][:, :]),
              r=[BF["var"], BF["rstd2"]], w=[BF["rstd2"]])
            E("scalar", lambda x: x.activation(out=S["rstd2"][:, :],
                                               in_=S["rstd2"][:, :],
                                               func=AF.Ln, bias=S["ceps"][:, 0:1]),
              s=[BF["rstd2"], BF["ceps"]], w=[BF["rstd2"]])
            E("scalar", lambda x: x.activation(out=S["rstd2"][:, :],
                                               in_=S["rstd2"][:, :],
                                               func=AF.Exp, scale=-0.5),
              s=[BF["rstd2"]], w=[BF["rstd2"]])
            E("vector", lambda x: x.tensor_mul(out=S["scomb"][:, :],
                                               in0=S["rstd"][:, :],
                                               in1=S["rstd2"][:, :]),
              r=[BF["rstd"], BF["rstd2"]], w=[BF["scomb"]])
            E("vector", lambda x: x.scalar_tensor_tensor(
                out=S["nms"][:, :], in0=S["mean"][:, :], scalar=-1.0,
                in1=S["scomb"][:, :], op0=ALU.mult, op1=ALU.mult),
              r=[BF["mean"]], s=[BF["scomb"]], w=[BF["nms"]])
            if not halo:
                E("scalar", lambda x: x.activation(
                    out=S["zscr"][:, :], in_=S["x1b"][:, :], func=AF.Identity,
                    bias=S["nmr"][:, 0:1], scale=S["rstd"][:, 0:1]),
                  r=[BF["x1b"]], s=[BF["nmr"], BF["rstd"]], w=[BF["zscr"]])
                E("vector", lambda x: x.tensor_add(out=S["xbuf"][:, :],
                                                   in0=S["x1b"][:, :],
                                                   in1=S["zscr"][:, :]),
                  r=[BF["x1b"], BF["zscr"]], w=[BF["xbuf"]])
                E("gpsimd", lambda x, c=c: x.dma_start(
                    out=t_x2p.ap()[c * TC:(c + 1) * TC, :], in_=S["xbuf"][:, :]),
                  r=[BF["xbuf"]], w=[dbuf["x2p"]], dma=True)
            E("scalar", lambda x: x.activation(out=S["bft"][:, :], in_=S["x1b"][:, :],
                                               func=AF.Identity, bias=S["nms"][:, 0:1],
                                               scale=S["scomb"][:, 0:1]),
              r=[BF["x1b"]], s=[BF["nms"], BF["scomb"]], w=[BF["bft"]])
            mk_z1T("bft")
            for j in range(ND):
                E("tensor", lambda x, j=j: x.matmul(
                    out=S["psS"][0:DIN, 0:TC],
                    lhsT=S["ipWT"][:, j * DIN:(j + 1) * DIN],
                    rhs=S["z1T"][:, j * 128:(j + 1) * 128],
                    start=(j == 0), stop=(j == ND - 1)),
                  r=[BF["ipWT"], BF["z1T"]], w=[BF["psS"]])
            slot = "cvxH" if halo else ("cvx%d" % (c % 3))
            E("vector", lambda x, slot=slot: x.tensor_scalar(
                out=S[slot][:, 1:129], in0=S["psS"][0:DIN, 0:TC],
                scalar1=S["ipb"][:, 0:1], scalar2=None, op0=ALU.add),
              r=[BF["psS"]], s=[BF["ipb"]], w=[BF[slot]])

        def scan_block(c, main):
            stname = "state" if main else "xinit"
            for n in range(NST):
                E("gpsimd", lambda x, n=n: x.dma_start(
                    out=S["sc1"][:, n * TC:(n + 1) * TC], in_=S["spb"][:, :]),
                  r=[BF["spb"]], w=[BF["sc1"]], dma=True)
            E("vector", lambda x: x.tensor_mul(out=S["sc1"][:, :], in0=S["sc1"][:, :],
                                               in1=S["Arep"][:, :]),
              r=[BF["sc1"], BF["Arep"]], w=[BF["sc1"]])
            E("scalar", lambda x: x.activation(out=S["sc2"][:, :], in_=S["sc1"][:, :],
                                               func=AF.Exp),
              r=[BF["sc1"]], w=[BF["sc2"]])
            if main:
                bsrc = S["dbc"][32:32 + NST, :]
                E("gpsimd", lambda x, bsrc=bsrc: x.dma_start(
                    out=S["sc3"][0:1, :], in_=bsrc),
                  r=[BF["dbc"]], w=[BF["sc3"]], dma=True)
                bcast(S["sc1"][:, :], S["sc3"][0:1, :], DIN,
                      [BF["sc3"]], [BF["sc1"]])
                for n in range(NST):
                    E("gpsimd", lambda x, n=n: x.dma_start(
                        out=S["sc3"][:, n * TC:(n + 1) * TC], in_=S["Gb"][:, :]),
                      r=[BF["Gb"]], w=[BF["sc3"]], dma=True)
                E("vector", lambda x: x.tensor_mul(out=S["sc3"][:, :],
                                                   in0=S["sc3"][:, :],
                                                   in1=S["sc1"][:, :]),
                  r=[BF["sc3"], BF["sc1"]], w=[BF["sc3"]])
            v2 = S["sc2"][:, :].rearrange("p (n t) -> p n t", t=TC)
            v3 = S["sc3"][:, :].rearrange("p (n t) -> p n t", t=TC)
            E("vector", lambda x, v2=v2, stname=stname: x.tensor_mul(
                out=S["t16"][:, :], in0=v2[:, :, 0], in1=S[stname][:, :]),
              r=[BF["sc2"], BF[stname]], w=[BF["t16"]])
            E("vector", lambda x, v2=v2: x.memset(v2[:, :, 0], 0.0),
              r=[BF["sc2"]], w=[BF["sc2"]])
            if main:
                E("vector", lambda x, v3=v3: x.tensor_add(
                    out=v3[:, :, 0], in0=v3[:, :, 0], in1=S["t16"][:, :]),
                  r=[BF["sc3"], BF["t16"]], w=[BF["sc3"]])
            else:
                E("vector", lambda x, v3=v3: x.tensor_copy(
                    out=v3[:, :, 0], in_=S["t16"][:, :]),
                  r=[BF["t16"]], w=[BF["sc3"]])
            E("vector", lambda x: x.tensor_tensor_scan(
                out=S["sc1"][:, :], data0=S["sc2"][:, :], data1=S["sc3"][:, :],
                initial=0.0, op0=ALU.mult, op1=ALU.add),
              r=[BF["sc2"], BF["sc3"]], w=[BF["sc1"]])
            v1 = S["sc1"][:, :].rearrange("p (n t) -> p n t", t=TC)
            E("vector", lambda x, v1=v1, stname=stname: x.tensor_copy(
                out=S[stname][:, :], in_=v1[:, :, TC - 1]),
              r=[BF["sc1"]], w=[BF[stname]])
            if main:
                csrc = S["dbc"][64:64 + NST, :]
                E("gpsimd", lambda x, csrc=csrc: x.dma_start(
                    out=S["sc3"][0:1, :], in_=csrc),
                  r=[BF["dbc"]], w=[BF["sc3"]], dma=True)
                bcast(S["sc2"][:, :], S["sc3"][0:1, :], DIN,
                      [BF["sc3"]], [BF["sc2"]])
            else:
                E("gpsimd", lambda x: x.dma_start(
                    out=S["zscr"][0:1, 0:NST * TC], in_=S["Cc"][:, :]),
                  r=[BF["Cc"]], w=[BF["zscr"]], dma=True)
                bcast(S["sc2"][:, :], S["zscr"][0:1, 0:NST * TC], DIN,
                      [BF["zscr"]], [BF["sc2"]])
            E("vector", lambda x: x.tensor_mul(out=S["sc1"][:, :], in0=S["sc1"][:, :],
                                               in1=S["sc2"][:, :]),
              r=[BF["sc1"], BF["sc2"]], w=[BF["sc1"]])
            nv = S["sc1"][:, :].rearrange("p (n t) -> p t n", t=TC)
            E("vector", lambda x, nv=nv: x.tensor_reduce(out=S["ybuf"][:, :], in_=nv,
                                                         axis=AX.X, op=ALU.add),
              r=[BF["sc1"]], w=[BF["ybuf"]])
            if main:
                E("vector", lambda x, c=c: x.scalar_tensor_tensor(
                    out=S["ystore"][:, c * TC:(c + 1) * TC], in0=S["ubuf"][:, :],
                    scalar=S["Dp"][:, 0:1], in1=S["ybuf"][:, :],
                    op0=ALU.mult, op1=ALU.add),
                  r=[BF["ubuf"], BF["ybuf"]], s=[BF["Dp"]], w=[BF["ystore"]])
            else:
                E("vector", lambda x, c=c: x.tensor_add(
                    out=S["ystore"][:, c * TC:(c + 1) * TC],
                    in0=S["ystore"][:, c * TC:(c + 1) * TC], in1=S["ybuf"][:, :]),
                  r=[BF["ystore"], BF["ybuf"]], w=[BF["ystore"]])

        def conv_scan(c):
            slot = "cvx%d" % (c % 3)
            if c > 0:
                prev = "cvx%d" % ((c - 1) % 3)
                E("vector", lambda x, slot=slot, prev=prev: x.tensor_copy(
                    out=S[slot][:, 0:1], in_=S[prev][:, 128:129]),
                  r=[BF[prev]], w=[BF[slot]])
            else:
                E("vector", lambda x, slot=slot: x.tensor_scalar_mul(
                    out=S[slot][:, 0:1], in0=S["cvxH"][:, 128:129],
                    scalar1=S["ml"][:, 0:1]),
                  r=[BF["cvxH"]], s=[BF["ml"]], w=[BF[slot]])
            if c < NCH - 1:
                nxt = "cvx%d" % ((c + 1) % 3)
                E("vector", lambda x, slot=slot, nxt=nxt: x.tensor_copy(
                    out=S[slot][:, 129:130], in_=S[nxt][:, 1:2]),
                  r=[BF[nxt]], w=[BF[slot]])
            else:
                E("vector", lambda x, slot=slot: x.tensor_scalar_mul(
                    out=S[slot][:, 129:130], in0=S["cvxH"][:, 1:2],
                    scalar1=S["mr"][:, 0:1]),
                  r=[BF["cvxH"]], s=[BF["mr"]], w=[BF[slot]])
            E("vector", lambda x, slot=slot: x.tensor_scalar_mul(
                out=S["cvA"][:, :], in0=S[slot][:, 0:TC],
                scalar1=S["convw"][:, 0:1]),
              r=[BF[slot]], s=[BF["convw"]], w=[BF["cvA"]])
            E("vector", lambda x, slot=slot: x.scalar_tensor_tensor(
                out=S["cvB"][:, :], in0=S[slot][:, 1:TC + 1],
                scalar=S["convw"][:, 1:2], in1=S["cvA"][:, :],
                op0=ALU.mult, op1=ALU.add),
              r=[BF[slot], BF["cvA"]], s=[BF["convw"]], w=[BF["cvB"]])
            E("vector", lambda x, slot=slot: x.scalar_tensor_tensor(
                out=S["cvA"][:, :], in0=S[slot][:, 2:TC + 2],
                scalar=S["convw"][:, 2:3], in1=S["cvB"][:, :],
                op0=ALU.mult, op1=ALU.add),
              r=[BF[slot], BF["cvB"]], s=[BF["convw"]], w=[BF["cvA"]])
            E("scalar", lambda x: x.activation(out=S["ubuf"][:, :], in_=S["cvA"][:, :],
                                               func=AF.Silu, bias=S["convb"][:, 0:1]),
              r=[BF["cvA"]], s=[BF["convb"]], w=[BF["ubuf"]])
            E("scalar", lambda x: x.activation(out=S["ub16"][:, :], in_=S["ubuf"][:, :],
                                               func=AF.Copy),
              r=[BF["ubuf"]], w=[BF["ub16"]])
            E("tensor", lambda x: x.matmul(out=S["psQ"][0:DIN, 0:TC],
                                           lhsT=S["xpWT"][:, :], rhs=S["ub16"][:, :],
                                           start=True, stop=True),
              r=[BF["xpWT"], BF["ub16"]], w=[BF["psQ"]])
            E("vector", lambda x: x.tensor_copy(out=S["dbc"][:, :],
                                                in_=S["psQ"][0:DIN, 0:TC]),
              r=[BF["psQ"]], w=[BF["dbc"]])
            E("scalar", lambda x: x.activation(out=S["dr6"][:, :],
                                               in_=S["dbc"][0:DTR, :], func=AF.Copy),
              r=[BF["dbc"]], w=[BF["dr6"]])
            E("tensor", lambda x: x.matmul(out=S["psQ"][0:DIN, 128:128 + TC],
                                           lhsT=S["dtWT"][:, :], rhs=S["dr6"][:, :],
                                           start=True, stop=True),
              r=[BF["dtWT"], BF["dr6"]], w=[BF["psQ"]])
            E("scalar", lambda x: x.activation(out=S["sca"][:, :],
                                               in_=S["psQ"][0:DIN, 128:128 + TC],
                                               func=AF.Relu, bias=S["dtb"][:, 0:1]),
              r=[BF["psQ"]], s=[BF["dtb"]], w=[BF["sca"]])
            E("scalar", lambda x: x.activation(out=S["scb"][:, :],
                                               in_=S["psQ"][0:DIN, 128:128 + TC],
                                               func=AF.Abs, bias=S["dtb"][:, 0:1]),
              r=[BF["psQ"]], s=[BF["dtb"]], w=[BF["scb"]])
            E("vector", lambda x: x.tensor_scalar_min(out=S["scb"][:, :],
                                                      in0=S["scb"][:, :],
                                                      scalar1=80.0),
              r=[BF["scb"]], w=[BF["scb"]])
            E("scalar", lambda x: x.activation(out=S["scb"][:, :], in_=S["scb"][:, :],
                                               func=AF.Exp, scale=-1.0),
              s=[BF["scb"]], w=[BF["scb"]])
            E("scalar", lambda x: x.activation(out=S["scb"][:, :], in_=S["scb"][:, :],
                                               func=AF.Ln, bias=S["cone"][0:DIN, 0:1]),
              s=[BF["scb"], BF["cone"]], w=[BF["scb"]])
            E("vector", lambda x: x.tensor_add(out=S["spb"][:, :], in0=S["sca"][:, :],
                                               in1=S["scb"][:, :]),
              r=[BF["sca"], BF["scb"]], w=[BF["spb"]])
            E("vector", lambda x: x.tensor_mul(out=S["Gb"][:, :], in0=S["spb"][:, :],
                                               in1=S["ubuf"][:, :]),
              r=[BF["spb"], BF["ubuf"]], w=[BF["Gb"]])
            E("gpsimd", lambda x, c=c: x.dma_start(
                out=t_dsc.ap()[:, c * TC:(c + 1) * TC], in_=S["spb"][:, :]),
              r=[BF["spb"]], w=[dbuf["dsc"]], dma=True)
            E("gpsimd", lambda x, c=c: x.dma_start(
                out=t_csc.ap()[:, c * TC:(c + 1) * TC], in_=S["dbc"][64:64 + NST, :]),
              r=[BF["dbc"]], w=[dbuf["csc"]], dma=True)
            scan_block(c, True)

        pass1_chunk(0, halo=True)
        for c in range(NCH):
            pass1_chunk(c, halo=False)
            if c == 0:
                dbg("z1f0", S["z1f"][:, 0:256], [BF["z1f"]])
                dbg("x1b0", S["x1b"][:, 0:256], [BF["x1b"]])
                dbg("xbuf0", S["xbuf"][:, 0:256], [BF["xbuf"]])
            if c >= 1:
                conv_scan(c - 1)
            if c == 1:
                dbg("ip0", S["cvx0"][:, 0:130], [BF["cvx0"]])
                dbg("u0", S["ubuf"][:, 0:128], [BF["ubuf"]])
                dbg("sp0", S["spb"][:, 0:128], [BF["spb"]])
                dbg("dbc0", S["dbc"][:, 0:128], [BF["dbc"]])
                dbg("y0", S["ystore"][:, 0:128], [BF["ystore"]])
        conv_scan(NCH - 1)

        E("gpsimd", lambda x: x.dma_start(
            out=t_sti.ap().rearrange("a (p n) -> (a p) n", p=DIN),
            in_=S["state"][:, :]),
          r=[BF["state"]], w=[dbuf["sti"]], dma=True)
        E("gpsimd", lambda x: x.collective_compute(
            "AllGather", ALU.bypass, replica_groups=[list(range(8))],
            ins=[t_sti.ap()], outs=[t_sto.ap()]),
          r=[dbuf["sti"]], w=[dbuf["sto"]], cc=True)
        ld(S["zscr"][0:8, 0:DIN * NST], t_sto.ap(), [dbuf["sto"]], [BF["zscr"]])
        for i in range(3):
            pw = ("psW0", "psW1", "psS")[i]
            E("tensor", lambda x, i=i, pw=pw: x.matmul(
                out=S[pw][0:1, 0:512], lhsT=S["psel"][:, :],
                rhs=S["zscr"][0:8, i * 512:(i + 1) * 512], start=True, stop=True),
              r=[BF["psel"], BF["zscr"]], w=[BF[pw]])
            E("vector", lambda x, i=i, pw=pw: x.tensor_copy(
                out=S["zscr"][32:33, i * 512:(i + 1) * 512], in_=S[pw][0:1, 0:512]),
              r=[BF[pw]], w=[BF["zscr"]])
        E("gpsimd", lambda x: x.dma_start(out=t_xirow.ap(),
                                          in_=S["zscr"][32:33, 0:DIN * NST]),
          r=[BF["zscr"]], w=[dbuf["xirow"]], dma=True)
        ld(S["xinit"][:, :],
           t_xirow.ap().rearrange("a (p n) -> (a p) n", p=DIN),
           [dbuf["xirow"]], [BF["xinit"]])

        E("gpsimd", lambda x: x.memset(S["sc3"][:, :], 0.0), w=[BF["sc3"]])
        for c in range(NCH):
            ld(S["spb"][:, :], t_dsc.ap()[:, c * TC:(c + 1) * TC],
               [dbuf["dsc"]], [BF["spb"]])
            ld(S["Cc"][:, :], t_csc.ap()[:, c * TC:(c + 1) * TC],
               [dbuf["csc"]], [BF["Cc"]])
            scan_block(c, False)

        def oproj_chunk(c, dst_dram, dst_buf):
            ld(S["xbuf"][:, :], t_x2p.ap()[c * TC:(c + 1) * TC, :],
               [dbuf["x2p"]], [BF["xbuf"]])
            E("scalar", lambda x, c=c: x.activation(
                out=S["ybf"][:, :], in_=S["ystore"][:, c * TC:(c + 1) * TC],
                func=AF.Copy),
              r=[BF["ystore"]], w=[BF["ybf"]])
            for fc in range(4):
                pw = "psW0" if fc % 2 == 0 else "psW1"
                E("tensor", lambda x, fc=fc, pw=pw: x.matmul(
                    out=S[pw][:, :], lhsT=S["ybf"][:, :],
                    rhs=S["opWT"][:, fc * 512:(fc + 1) * 512],
                    start=True, stop=True),
                  r=[BF["ybf"], BF["opWT"]], w=[BF[pw]])
                E("vector", lambda x, fc=fc, pw=pw: x.tensor_add(
                    out=S["x1b"][:, fc * 512:(fc + 1) * 512], in0=S[pw][:, :],
                    in1=S["xbuf"][:, fc * 512:(fc + 1) * 512]),
                  r=[BF[pw], BF["xbuf"]], w=[BF["x1b"]])
            if dst_dram is not None:
                E("gpsimd", lambda x, c=c, d=dst_dram: x.dma_start(
                    out=d.ap()[c * TC:(c + 1) * TC, :], in_=S["x1b"][:, :]),
                  r=[BF["x1b"]], w=[dst_buf], dma=True)

        for c in range(16):
            oproj_chunk(c, t_x2b, dbuf["x2b"])

        def _a2a(x):
            with allow_ici_all_to_all():
                return x.collective_compute(
                    "AllToAll", ALU.bypass, replica_groups=[list(range(8))],
                    ins=[t_x2b.ap()], outs=[t_a2a.ap()])
        E("gpsimd", _a2a, r=[dbuf["x2b"]], w=[dbuf["a2a"]], cc=True)

        def a2a_rows(b, i):
            r0 = (2 * b) * 256 + i * 128
            return t_a2a.ap()[r0:r0 + 128, :]
        router(a2a_rows, dbuf["a2a"], t_l3i, dbuf["l3i"], t_l3o, dbuf["l3o"])
        selection()

        def pass2_chunk(c):
            if c < 16:
                ld(S["x1b"][:, :], t_x2b.ap()[c * TC:(c + 1) * TC, :],
                   [dbuf["x2b"]], [BF["x1b"]])
            else:
                oproj_chunk(c, None, None)
            ln_stats("x1b")
            E("scalar", lambda x: x.activation(out=S["z1f"][:, :], in_=S["x1b"][:, :],
                                               func=AF.Identity, bias=S["nmr"][:, 0:1],
                                               scale=S["rstd"][:, 0:1]),
              r=[BF["x1b"]], s=[BF["nmr"], BF["rstd"]], w=[BF["z1f"]])
            E("scalar", lambda x: x.activation(out=S["bft"][:, :], in_=S["z1f"][:, :],
                                               func=AF.Copy),
              r=[BF["z1f"]], w=[BF["bft"]])
            mk_z1T("bft")
            ffn_body("z1f", "x1b", "zscr")
            E("gpsimd", lambda x, c=c: x.dma_start(
                out=t_yout.ap()[c * TC:(c + 1) * TC, :], in_=S["zscr"][:, :]),
              r=[BF["zscr"]], w=[dbuf["yout"]], dma=True)

        for c in range(NCH):
            pass2_chunk(c)

        final_dout = kb.cnt["dout"]
        final_din = kb.cnt["din"]

        def emit(engobj, engname):
            emitted = {}
            for fn, waits, sem, d in kb.ops[engname]:
                for smn, v in waits:
                    if emitted.get(smn, 0) < v:
                        engobj.wait_ge(sems[smn], v)
                        emitted[smn] = v
                instr = fn(engobj)
                instr.then_inc(sems[sem], d)

        @block.sync
        def _(sync):
            emit(sync, "sync")

        @block.vector
        def _(vector):
            emit(vector, "vector")

        @block.scalar
        def _(scalar):
            emit(scalar, "scalar")

        @block.tensor
        def _(tensor):
            emit(tensor, "tensor")

        @block.gpsimd
        def _(gpsimd):
            emit(gpsimd, "gpsimd")
            gpsimd.wait_ge(sems["dout"], final_dout)
            gpsimd.wait_ge(sems["din"], final_din)

    return nc


_NC_CACHE = None


def kernel(**inputs):
    global _NC_CACHE
    f32 = np.float32
    x = np.ascontiguousarray(inputs["L0_attn_feat_patch"], f32)
    sw = np.ascontiguousarray(inputs["L0_ffn_switch_w"], f32)
    ident = np.eye(128).astype(ml_dtypes.bfloat16)
    if _NC_CACHE is None:
        _NC_CACHE = build_program()
    nc = _NC_CACHE

    in_maps = []
    for core in range(8):
        b, h = core // 2, core % 2
        xo = x[b, h * HALF:(h + 1) * HALF]
        if h == 0:
            halorows = x[b, HALF:HALF + TC]
        else:
            halorows = x[b, HALF - TC:HALF]
        xloc = np.concatenate([xo, halorows], axis=0)
        rx = np.ascontiguousarray(
            x[:, core * 256:(core + 1) * 256, :]).reshape(1024, D)
        swsl = np.ascontiguousarray(
            sw.reshape(8, 2048, D)[:, core * 256:(core + 1) * 256, :]
        ).reshape(8, 524288)
        bsel = np.zeros((4, 1), f32)
        bsel[b, 0] = 1.0
        psel = np.zeros((8, 1), f32)
        if h == 1:
            psel[2 * b, 0] = 1.0
        ml = np.full((DIN, 1), 1.0 if h == 1 else 0.0, f32)
        mr = np.full((DIN, 1), 1.0 if h == 0 else 0.0, f32)
        in_maps.append({
            "xloc": np.ascontiguousarray(xloc),
            "rx": rx,
            "swsl": swsl,
            "swb": np.ascontiguousarray(inputs["L0_ffn_switch_b"], f32).reshape(1, 8),
            "w1": np.ascontiguousarray(inputs["L0_ffn_w1"], f32),
            "b1": np.ascontiguousarray(inputs["L0_ffn_b1"], f32),
            "w2": np.ascontiguousarray(inputs["L0_ffn_w2"], f32),
            "b2": np.ascontiguousarray(inputs["L0_ffn_b2"], f32),
            "w3": np.ascontiguousarray(inputs["L0_ffn_w3"], f32),
            "b3": np.ascontiguousarray(inputs["L0_ffn_b3"], f32).reshape(1, D),
            "lnw": np.ascontiguousarray(inputs["L0_attn_ln_w"], f32).reshape(1, D),
            "lnb": np.ascontiguousarray(inputs["L0_attn_ln_b"], f32).reshape(1, D),
            "ipW": np.ascontiguousarray(inputs["L0_attn_in_proj_W"], f32),
            "convw": np.ascontiguousarray(
                inputs["L0_attn_conv_w"], f32).reshape(DIN, 3),
            "convb": np.ascontiguousarray(
                inputs["L0_attn_conv_b"], f32).reshape(DIN, 1),
            "xpW": np.ascontiguousarray(inputs["L0_attn_x_proj_W"], f32),
            "dtW": np.ascontiguousarray(inputs["L0_attn_dt_proj_W"], f32),
            "dtb": np.ascontiguousarray(
                inputs["L0_attn_dt_proj_b"], f32).reshape(DIN, 1),
            "Alog": np.ascontiguousarray(inputs["L0_attn_A_log"], f32),
            "Dp": np.ascontiguousarray(
                inputs["L0_attn_D_param"], f32).reshape(DIN, 1),
            "opW": np.ascontiguousarray(inputs["L0_attn_out_proj_W"], f32),
            "ident": ident,
            "bsel": bsel,
            "psel": psel,
            "ml": ml,
            "mr": mr,
        })

    with allow_ici_all_to_all():
        res = run_bass_kernel_spmd(nc, in_maps, core_ids=list(range(8)))

    out = np.empty((B_, L_, D), f32)
    for core in range(8):
        b, h = core // 2, core % 2
        out[b, h * HALF:(h + 1) * HALF] = res.results[core]["yout"]
    return out


# version 07 (final)
# speedup vs baseline: 1.0631x; 1.0631x over previous
import numpy as np
import ml_dtypes
from contextlib import ExitStack
import concourse.bass as bass
import concourse.mybir as mybir
from concourse.bass import AP
from concourse.bass_utils import run_bass_kernel_spmd
from concourse.aot_env import allow_ici_all_to_all

F32 = mybir.dt.float32
BF16 = mybir.dt.bfloat16
AF = mybir.ActivationFunctionType
ALU = mybir.AluOpType
AX = mybir.AxisListType

B_, L_, D = 4, 8192, 2048
TC = 128
NCH = 32
ND = 16
DIN = 96
NST = 16
DTR = 6
EPS = 1e-5
HALF = 4096

ENGINES = ("sync", "vector", "scalar", "tensor", "gpsimd")


class Buf:
    __slots__ = ("w", "r")
    def __init__(self):
        self.w = None
        self.r = []


class KB:
    ENG_SEM = {"tensor": "pe", "vector": "dve", "scalar": "act",
               "gpsimd": "pool", "sync": "snc"}
    SEMS = ("pe", "dve", "act", "pool", "snc", "din", "dout", "cc")

    def __init__(self):
        self.ops = {e: [] for e in ENGINES}
        self.cnt = {s: 0 for s in self.SEMS}

    def op(self, eng, fn, r=(), w=(), s=(), dma=False, cc=False):
        waits = {}

        def need(ev, hard):
            if ev is None:
                return
            sm, v = ev
            if (not hard) and (not dma) and (not cc) \
                    and sm == self.ENG_SEM[eng] and sm == "pe":
                return
            if waits.get(sm, 0) < v:
                waits[sm] = v

        for b in r:
            need(b.w, False)
        for b in s:
            need(b.w, True)
        for b in w:
            need(b.w, True)
            for ev in b.r:
                need(ev, False)
        if cc:
            sem, d = "cc", 1
        elif dma:
            sem, d = ("din" if eng == "sync" else "dout"), 16
        else:
            sem, d = self.ENG_SEM[eng], 1
        self.cnt[sem] += d
        ev = (sem, self.cnt[sem])
        self.ops[eng].append((fn, sorted(waits.items()), sem, d))
        for b in tuple(r) + tuple(s):
            b.r.append(ev)
        for b in w:
            b.w = ev
            b.r = []
        return ev


def build_program():
    nc = bass.Bass("TRN2", target_bir_lowering=False)
    kb = KB()

    def dram_in(name, shape, dtype=F32):
        return nc.dram_tensor(name, shape, dtype, kind="ExternalInput")

    t_xloc = dram_in("xloc", [4224, D])
    t_rx = dram_in("rx", [1024, D])
    t_sw = dram_in("swsl", [8, 524288])
    t_swb = dram_in("swb", [1, 8])
    t_w1 = dram_in("w1", [8, 128, 128])
    t_b1 = dram_in("b1", [8, 128])
    t_w2 = dram_in("w2", [8, 128, 128])
    t_b2 = dram_in("b2", [8, 128])
    t_w3 = dram_in("w3", [D, D])
    t_b3 = dram_in("b3", [1, D])
    t_lnw = dram_in("lnw", [1, D])
    t_lnb = dram_in("lnb", [1, D])
    t_ipW = dram_in("ipW", [DIN, D])
    t_convw = dram_in("convw", [DIN, 3])
    t_convb = dram_in("convb", [DIN, 1])
    t_xpW = dram_in("xpW", [38, DIN])
    t_dtW = dram_in("dtW", [DIN, DTR])
    t_dtb = dram_in("dtb", [DIN, 1])
    t_Alog = dram_in("Alog", [DIN, NST])
    t_Dp = dram_in("Dp", [DIN, 1])
    t_opW = dram_in("opW", [D, DIN])
    t_ident = dram_in("ident", [128, 128], BF16)
    t_bsel = dram_in("bsel", [4, 1])
    t_psel = dram_in("psel", [8, 1])
    t_ml = dram_in("ml", [DIN, 1])
    t_mr = dram_in("mr", [DIN, 1])

    t_yout = nc.dram_tensor("yout", [HALF, D], F32, kind="ExternalOutput")

    t_x2p = nc.dram_tensor("x2p", [HALF, D], F32)
    t_x2b = nc.dram_tensor("x2b", [2048, D], F32)
    t_a2a = nc.dram_tensor("a2a", [2048, D], F32)
    t_dsc = nc.dram_tensor("dsc", [DIN, HALF], F32)
    t_csc = nc.dram_tensor("csc", [NST, HALF], F32)
    t_l1i = nc.dram_tensor("l1i", [4, 8], F32)
    t_l1o = nc.dram_tensor("l1o", [4, 8], F32)
    t_sti = nc.dram_tensor("sti", [1, DIN * NST], F32)
    t_sto = nc.dram_tensor("sto", [8, DIN * NST], F32)
    t_l3i = nc.dram_tensor("l3i", [4, 8], F32)
    t_l3o = nc.dram_tensor("l3o", [4, 8], F32)
    t_tiny = nc.dram_tensor("tiny", [1, 64], F32)
    import os
    KDBG = bool(os.environ.get("KDBG"))
    t_dbg = nc.dram_tensor("dbg", [128, 16384], F32, kind="ExternalOutput") \
        if KDBG else None
    dbg_reg = {}
    dbg_off = [0]
    t_xirow = nc.dram_tensor("xirow", [1, DIN * NST], F32)

    dbuf = {t: Buf() for t in ("xloc", "rx", "sw", "yout", "x2p", "x2b", "a2a",
                               "dsc", "csc", "l1i", "l1o", "sti", "sto",
                               "l3i", "l3o", "tiny", "xirow")}

    with ExitStack() as stack:
        en = stack.enter_context
        S = {}
        BF = {}

        def sb(name, shape, dtype=F32):
            S[name] = en(nc.sbuf_tensor("s_" + name, shape, dtype))
            BF[name] = Buf()

        def psum(name, shape, dtype=F32):
            S[name] = en(nc.psum_tensor("p_" + name, shape, dtype))
            BF[name] = Buf()

        sb("w3T", [128, ND * D], BF16)
        sb("opWT", [DIN, D], BF16)
        sb("ipWT", [128, ND * DIN], BF16)
        sb("xpWT", [DIN, DIN], BF16)
        sb("dtWT", [DTR, DIN], BF16)
        sb("ident", [128, 128], BF16)
        sb("onesb", [1, 512], BF16)
        sb("onesc", [128, 1], F32)
        sb("b3p", [1, D], BF16)
        sb("App", [DIN, NST], F32)
        sb("Arep", [DIN, NST * TC], F32)
        sb("Dp", [DIN, 1], F32)
        sb("convw", [DIN, 3], F32)
        sb("convb", [DIN, 1], F32)
        sb("dtb", [DIN, 1], F32)
        sb("ipb", [DIN, 1], F32)
        sb("lnw", [1, D], F32)
        sb("lnbcols", [128, ND], F32)
        sb("lnbb", [128, ND], BF16)
        sb("swb", [1, 8], F32)
        sb("bsel", [4, 1], F32)
        sb("psel", [8, 1], F32)
        sb("ml", [DIN, 1], F32)
        sb("mr", [DIN, 1], F32)
        sb("b1s", [8, 128], F32)
        sb("b2s", [8, 128], F32)
        sb("w1all", [128, 8 * 128], F32)
        sb("w2all", [128, 8 * 128], F32)
        sb("w1T0", [128, 128], BF16)
        sb("w1T1", [128, 128], BF16)
        sb("w2T0", [128, 128], BF16)
        sb("w2T1", [128, 128], BF16)
        sb("b1r0", [1, 128], BF16)
        sb("b1r1", [1, 128], BF16)
        sb("b2r4", [1, 512], BF16)
        sb("ws0", [128, 128], F32)
        sb("ws1", [128, 128], F32)
        sb("wsb", [128, 128], BF16)
        sb("mrow", [128, 8], F32)
        sb("xbuf", [128, D], F32)
        sb("z1f", [128, D], F32)
        sb("z1b", [128, D], BF16)
        sb("x1b", [128, D], F32)
        sb("zscr", [128, D], F32)
        sb("bft", [128, D], BF16)
        sb("z1T", [128, D], BF16)
        sb("zoT", [128, D], BF16)
        sb("hb0", [128, 512], BF16)
        sb("hb1", [128, 512], BF16)
        sb("sc1", [DIN, NST * TC], F32)
        sb("sc2", [DIN, NST * TC], F32)
        sb("sc3", [DIN, NST * TC], F32)
        sb("racc", [128, 8], F32)
        sb("racc2", [128, 8], F32)
        sb("cvx0", [DIN, 130], F32)
        sb("cvx1", [DIN, 130], F32)
        sb("cvx2", [DIN, 130], F32)
        sb("cvxH", [DIN, 130], F32)
        sb("cvA", [DIN, TC], F32)
        sb("cvB", [DIN, TC], F32)
        sb("ubuf", [DIN, TC], F32)
        sb("ub16", [DIN, TC], BF16)
        sb("dbc", [DIN, TC], F32)
        sb("dr6", [DTR, TC], BF16)
        sb("sca", [DIN, TC], F32)
        sb("scb", [DIN, TC], F32)
        sb("spb", [DIN, TC], F32)
        sb("Gb", [DIN, TC], F32)
        sb("ybuf", [DIN, TC], F32)
        sb("Cc", [NST, TC], F32)
        sb("t16", [DIN, NST], F32)
        sb("state", [DIN, NST], F32)
        sb("xinit", [DIN, NST], F32)
        sb("ystore", [DIN, HALF], F32)
        sb("ybf", [DIN, TC], BF16)
        sb("s1c", [128, 1], F32)
        sb("s2c", [128, 1], F32)
        sb("mean", [128, 1], F32)
        sb("var", [128, 1], F32)
        sb("rstd", [128, 1], F32)
        sb("nmr", [128, 1], F32)
        sb("rstd2", [128, 1], F32)
        sb("scomb", [128, 1], F32)
        sb("nms", [128, 1], F32)
        sb("tiny1", [128, 1], F32)
        sb("ceps", [128, 1], F32)
        sb("cone", [128, 1], F32)
        sb("lg", [4, 8], F32)
        sb("swbb", [4, 8], F32)
        sb("lrow", [1, 8], F32)
        sb("nmx", [1, 1], F32)
        sb("ssum", [1, 1], F32)
        sb("rsum", [1, 1], F32)
        sb("probs", [1, 8], F32)
        sb("m1", [1, 8], F32)
        sb("m2", [1, 8], F32)
        sb("p2", [1, 8], F32)
        sb("mx1", [1, 1], F32)
        sb("mx2", [1, 1], F32)
        sb("g1", [1, 1], F32)
        sb("g2", [1, 1], F32)
        sb("j8", [1, 8], F32)
        sb("coef", [1, 8], F32)
        sb("coefb", [1, 8], F32)
        sb("m1T", [8, 1], F32)
        sb("m2T", [8, 1], F32)
        sb("coefT", [8, 1], F32)
        sb("sgrow", [8, 4], F32)

        psum("psA", [128, 512], F32)
        psum("psB", [128, 512], F32)
        psum("psW0", [128, 512], F32)
        psum("psW1", [128, 512], F32)
        psum("psS", [128, 512], F32)
        psum("psQ", [128, 512], F32)
        psum("psT", [128, 1024], BF16)
        psum("psT2", [128, 1024], BF16)

        S["rscr"] = S["bft"]; BF["rscr"] = BF["bft"]
        S["zob"] = S["bft"]; BF["zob"] = BF["bft"]
        S["rsw"] = S["x1b"]; BF["rsw"] = BF["x1b"]
        S["rsw2"] = S["z1f"]; BF["rsw2"] = BF["z1f"]
        S["rxt"] = S["xbuf"]; BF["rxt"] = BF["xbuf"]
        S["rprod"] = S["xbuf"]; BF["rprod"] = BF["xbuf"]
        S["rpart"] = S["tiny1"]; BF["rpart"] = BF["tiny1"]

        sems = {}
        for smn in KB.SEMS:
            sems[smn] = en(nc.semaphore(smn))
        block = en(nc.Block())

        def E(eng, fn, r=(), w=(), s=(), dma=False, cc=False):
            return kb.op(eng, fn, r=r, w=w, s=s, dma=dma, cc=cc)

        dbgbuf = Buf()

        def dbg(name, ap, rbufs):
            if not KDBG:
                return
            p, f = ap.shape[-2], ap.shape[-1]
            off = dbg_off[0]
            dbg_reg[name] = (off, p, f)
            dbg_off[0] += f
            E("gpsimd", lambda x, ap=ap, off=off, p=p, f=f: x.dma_start(
                out=t_dbg.ap()[0:p, off:off + f], in_=ap,
                allow_slow_non_contiguous=True),
              r=list(rbufs), w=[dbgbuf], dma=True)
        build_program.dbg_reg = dbg_reg

        def ld(dst_ap, src_ap, rbufs, wbufs, eng="sync", slow=False):
            if slow:
                E(eng, lambda x, d=dst_ap, so=src_ap: x.dma_start(
                    out=d, in_=so, allow_slow_non_contiguous=True),
                  r=rbufs, w=wbufs, dma=True)
            else:
                E(eng, lambda x, d=dst_ap, so=src_ap: x.dma_start(out=d, in_=so),
                  r=rbufs, w=wbufs, dma=True)

        def bcast(dst_ap, src_row_ap, nrep, rbufs, wbufs, eng="gpsimd"):
            f = src_row_ap.shape[-1]
            src = AP(src_row_ap.tensor, src_row_ap.offset, [[1, 1], [0, nrep], [1, f]])
            E(eng, lambda x, d=dst_ap, so=src: x.dma_start(out=d, in_=so),
              r=rbufs, w=wbufs, dma=True)

        def repn(dst_ap, src_2d_ap, p, rep, f, rbufs, wbufs, eng="gpsimd"):
            src = AP(src_2d_ap.tensor, src_2d_ap.offset, [[1, p], [0, rep], [1, f]])
            E(eng, lambda x, d=dst_ap, so=src: x.dma_start(out=d, in_=so),
              r=rbufs, w=wbufs, dma=True)

        def transpose16(srcname, dstname):
            for j in range(ND):
                pst = "psT" if (j % 16) < 8 else "psT2"
                slot = (j % 8) * 128
                E("tensor", lambda x, j=j, pst=pst, slot=slot: x.transpose(
                    out=S[pst][:, slot:slot + 128],
                    in_=S[srcname][:, j * 128:(j + 1) * 128],
                    identity=S["ident"][:, :]),
                  r=[BF[srcname], BF["ident"]], w=[BF[pst]])
                if j % 8 == 7:
                    pst2 = "psT" if j < 8 else "psT2"
                    j0 = j - 7
                    E("vector", lambda x, j0=j0, pst2=pst2, dstname=dstname:
                      x.tensor_copy(out=S[dstname][:, j0 * 128:(j0 + 8) * 128],
                                    in_=S[pst2][:, :]),
                      r=[BF[pst2]], w=[BF[dstname]])

        ld(S["ident"][:, :], t_ident.ap(), [], [BF["ident"]])
        ld(S["b1s"][:, :], t_b1.ap(), [], [BF["b1s"]])
        ld(S["b2s"][:, :], t_b2.ap(), [], [BF["b2s"]])
        ld(S["swb"][:, :], t_swb.ap(), [], [BF["swb"]])
        ld(S["bsel"][:, :], t_bsel.ap(), [], [BF["bsel"]])
        ld(S["psel"][:, :], t_psel.ap(), [], [BF["psel"]])
        ld(S["ml"][:, :], t_ml.ap(), [], [BF["ml"]])
        ld(S["mr"][:, :], t_mr.ap(), [], [BF["mr"]])
        ld(S["convw"][:, :], t_convw.ap(), [], [BF["convw"]])
        ld(S["convb"][:, :], t_convb.ap(), [], [BF["convb"]])
        ld(S["dtb"][:, :], t_dtb.ap(), [], [BF["dtb"]])
        ld(S["Dp"][:, :], t_Dp.ap(), [], [BF["Dp"]])
        ld(S["lnw"][:, :], t_lnw.ap(), [], [BF["lnw"]])
        ld(S["lnbcols"][:, :],
           t_lnb.ap().rearrange("a (k p) -> (a p) k", p=128), [], [BF["lnbcols"]],
           slow=True)
        for ee in range(8):
            ld(S["w1all"][:, ee * 128:(ee + 1) * 128], t_w1.ap()[ee], [], [BF["w1all"]])
            ld(S["w2all"][:, ee * 128:(ee + 1) * 128], t_w2.ap()[ee], [], [BF["w2all"]])
        E("gpsimd", lambda x: x.memset(S["onesb"][:, :], 1.0), w=[BF["onesb"]])
        E("gpsimd", lambda x: x.memset(S["onesc"][:, :], 1.0), w=[BF["onesc"]])
        E("gpsimd", lambda x: x.memset(S["ceps"][:, :], EPS), w=[BF["ceps"]])
        E("gpsimd", lambda x: x.memset(S["cone"][:, :], 1.0), w=[BF["cone"]])
        ld(S["App"][:, :], t_Alog.ap(), [], [BF["App"]])
        E("scalar", lambda x: x.activation(out=S["App"][:, :], in_=S["App"][:, :],
                                           func=AF.Exp),
          s=[BF["App"]], w=[BF["App"]])
        E("vector", lambda x: x.tensor_scalar_mul(out=S["App"][:, :],
                                                  in0=S["App"][:, :], scalar1=-0.001),
          r=[BF["App"]], w=[BF["App"]])
        E("gpsimd", lambda x: x.memset(S["cvA"][:, :], 1.0), w=[BF["cvA"]])
        for n in range(NST):
            E("vector", lambda x, n=n: x.tensor_scalar_mul(
                out=S["Arep"][:, n * TC:(n + 1) * TC], in0=S["cvA"][:, :],
                scalar1=S["App"][:, n:n + 1]),
              r=[BF["cvA"]], s=[BF["App"]], w=[BF["Arep"]])
        ld(S["zscr"][0:1, :], t_b3.ap(), [], [BF["zscr"]])
        E("scalar", lambda x: x.activation(out=S["b3p"][:, :], in_=S["zscr"][0:1, :],
                                           func=AF.Copy, scale=0.5),
          r=[BF["zscr"]], w=[BF["b3p"]])

        for j in range(ND):
            ld(S["zscr"][:, :], t_w3.ap()[j * 128:(j + 1) * 128, :], [], [BF["zscr"]])
            E("scalar", lambda x: x.activation(
                out=S["bft"][:, :], in_=S["zscr"][:, :], func=AF.Copy, scale=0.5),
              r=[BF["zscr"]], w=[BF["bft"]])
            for k in range(ND):
                pst = "psT" if k < 8 else "psT2"
                slot = (k % 8) * 128
                E("tensor", lambda x, k=k, pst=pst, slot=slot: x.transpose(
                    out=S[pst][:, slot:slot + 128],
                    in_=S["bft"][:, k * 128:(k + 1) * 128],
                    identity=S["ident"][:, :]),
                  r=[BF["bft"], BF["ident"]], w=[BF[pst]])
                if k % 8 == 7:
                    pst2 = "psT" if k < 8 else "psT2"
                    k0 = k - 7
                    w3v = S["w3T"][:, :].rearrange("p (k c) -> p k c", c=D)
                    E("vector", lambda x, j=j, k0=k0, pst2=pst2, w3v=w3v: x.tensor_copy(
                        out=w3v[:, k0:k0 + 8, j * 128:(j + 1) * 128],
                        in_=S[pst2][:, :].rearrange("p (k c) -> p k c", c=128)),
                      r=[BF[pst2]], w=[BF["w3T"]])

        for j in range(ND):
            ld(S["zscr"][:, 0:DIN], t_opW.ap()[j * 128:(j + 1) * 128, :],
               [], [BF["zscr"]])
            E("scalar", lambda x: x.activation(
                out=S["bft"][:, 0:DIN], in_=S["zscr"][:, 0:DIN], func=AF.Copy),
              r=[BF["zscr"]], w=[BF["bft"]])
            pst = "psT" if (j % 2) == 0 else "psT2"
            E("tensor", lambda x, pst=pst: x.transpose(
                out=S[pst][0:DIN, 0:128], in_=S["bft"][:, 0:DIN],
                identity=S["ident"][:, :]),
              r=[BF["bft"], BF["ident"]], w=[BF[pst]])
            E("vector", lambda x, j=j, pst=pst: x.tensor_copy(
                out=S["opWT"][:, j * 128:(j + 1) * 128], in_=S[pst][0:DIN, 0:128]),
              r=[BF[pst]], w=[BF["opWT"]])

        ld(S["zscr"][0:DIN, :], t_ipW.ap(), [], [BF["zscr"]])
        E("scalar", lambda x: x.activation(
            out=S["bft"][0:DIN, :], in_=S["zscr"][0:DIN, :], func=AF.Copy),
          r=[BF["zscr"]], w=[BF["bft"]])
        for j in range(ND):
            pst = "psT" if (j % 2) == 0 else "psT2"
            E("tensor", lambda x, j=j, pst=pst: x.transpose(
                out=S[pst][:, 0:DIN], in_=S["bft"][0:DIN, j * 128:(j + 1) * 128],
                identity=S["ident"][0:DIN, 0:DIN]),
              r=[BF["bft"], BF["ident"]], w=[BF[pst]])
            E("vector", lambda x, j=j, pst=pst: x.tensor_copy(
                out=S["ipWT"][:, j * DIN:(j + 1) * DIN], in_=S[pst][:, 0:DIN]),
              r=[BF[pst]], w=[BF["ipWT"]])
        E("scalar", lambda x: x.activation(out=S["lnbb"][:, :],
                                           in_=S["lnbcols"][:, :], func=AF.Copy),
          r=[BF["lnbcols"]], w=[BF["lnbb"]])
        for j in range(ND):
            E("tensor", lambda x, j=j: x.matmul(
                out=S["psS"][0:DIN, 256:257],
                lhsT=S["ipWT"][:, j * DIN:(j + 1) * DIN],
                rhs=S["lnbb"][:, j:j + 1],
                start=(j == 0), stop=(j == ND - 1)),
              r=[BF["ipWT"], BF["lnbb"]], w=[BF["psS"]])
        E("vector", lambda x: x.tensor_copy(out=S["ipb"][:, :],
                                            in_=S["psS"][0:DIN, 256:257]),
          r=[BF["psS"]], w=[BF["ipb"]])
        bcast(S["x1b"][0:DIN, :], S["lnw"][0:1, :], DIN, [BF["lnw"]], [BF["x1b"]])
        E("vector", lambda x: x.tensor_mul(out=S["zscr"][0:DIN, :],
                                           in0=S["zscr"][0:DIN, :],
                                           in1=S["x1b"][0:DIN, :]),
          r=[BF["zscr"], BF["x1b"]], w=[BF["zscr"]])
        E("scalar", lambda x: x.activation(
            out=S["bft"][0:DIN, :], in_=S["zscr"][0:DIN, :], func=AF.Copy),
          r=[BF["zscr"]], w=[BF["bft"]])
        for j in range(ND):
            pst = "psT" if (j % 2) == 0 else "psT2"
            E("tensor", lambda x, j=j, pst=pst: x.transpose(
                out=S[pst][:, 0:DIN], in_=S["bft"][0:DIN, j * 128:(j + 1) * 128],
                identity=S["ident"][0:DIN, 0:DIN]),
              r=[BF["bft"], BF["ident"]], w=[BF[pst]])
            E("vector", lambda x, j=j, pst=pst: x.tensor_copy(
                out=S["ipWT"][:, j * DIN:(j + 1) * DIN], in_=S[pst][:, 0:DIN]),
              r=[BF[pst]], w=[BF["ipWT"]])

        E("gpsimd", lambda x: x.memset(S["xpWT"][:, :], 0.0), w=[BF["xpWT"]])
        for (r0, nr, dst0, scale) in ((0, DTR, 0, 1.0), (DTR, NST, 32, 0.001),
                                      (DTR + NST, NST, 64, 1.0)):
            ld(S["zscr"][0:nr, 0:DIN], t_xpW.ap()[r0:r0 + nr, :], [], [BF["zscr"]])
            E("scalar", lambda x, nr=nr, scale=scale: x.activation(
                out=S["bft"][0:nr, 0:DIN], in_=S["zscr"][0:nr, 0:DIN],
                func=AF.Copy, scale=scale),
              r=[BF["zscr"]], w=[BF["bft"]])
            E("tensor", lambda x, nr=nr: x.transpose(
                out=S["psT"][0:DIN, 0:nr], in_=S["bft"][0:nr, 0:DIN],
                identity=S["ident"][0:nr, 0:nr]),
              r=[BF["bft"], BF["ident"]], w=[BF["psT"]])
            E("vector", lambda x, nr=nr, dst0=dst0: x.tensor_copy(
                out=S["xpWT"][:, dst0:dst0 + nr], in_=S["psT"][0:DIN, 0:nr]),
              r=[BF["psT"]], w=[BF["xpWT"]])

        ld(S["zscr"][0:DIN, 0:DTR], t_dtW.ap(), [], [BF["zscr"]])
        E("scalar", lambda x: x.activation(out=S["bft"][0:DIN, 0:DTR],
                                           in_=S["zscr"][0:DIN, 0:DTR], func=AF.Copy),
          r=[BF["zscr"]], w=[BF["bft"]])
        E("tensor", lambda x: x.transpose(out=S["psT"][0:DTR, 0:DIN],
                                          in_=S["bft"][0:DIN, 0:DTR],
                                          identity=S["ident"][0:DIN, 0:DIN]),
          r=[BF["bft"], BF["ident"]], w=[BF["psT"]])
        E("vector", lambda x: x.tensor_copy(out=S["dtWT"][:, :],
                                            in_=S["psT"][0:DTR, 0:DIN]),
          r=[BF["psT"]], w=[BF["dtWT"]])

        def ln_stats(src_name):
            E("scalar", lambda x: x.activation(out=S["zscr"][:, :],
                                               in_=S[src_name][:, :], func=AF.Copy,
                                               accum_out=S["s1c"][:, 0:1]),
              r=[BF[src_name]], w=[BF["zscr"], BF["s1c"]])
            E("scalar", lambda x: x.activation(out=S["zscr"][:, :],
                                               in_=S[src_name][:, :], func=AF.Square,
                                               accum_out=S["s2c"][:, 0:1]),
              r=[BF[src_name]], w=[BF["zscr"], BF["s2c"]])
            E("vector", lambda x: x.tensor_scalar_mul(out=S["mean"][:, :],
                                                      in0=S["s1c"][:, :],
                                                      scalar1=1.0 / D),
              r=[BF["s1c"]], w=[BF["mean"]])
            E("scalar", lambda x: x.activation(out=S["tiny1"][:, :],
                                               in_=S["mean"][:, :], func=AF.Square),
              s=[BF["mean"]], w=[BF["tiny1"]])
            E("vector", lambda x: x.tensor_scalar_mul(out=S["var"][:, :],
                                                      in0=S["s2c"][:, :],
                                                      scalar1=1.0 / D),
              r=[BF["s2c"]], w=[BF["var"]])
            E("vector", lambda x: x.tensor_sub(out=S["var"][:, :], in0=S["var"][:, :],
                                               in1=S["tiny1"][:, :]),
              r=[BF["var"], BF["tiny1"]], w=[BF["var"]])
            E("scalar", lambda x: x.activation(out=S["rstd"][:, :], in_=S["var"][:, :],
                                               func=AF.Ln, bias=S["ceps"][:, 0:1]),
              s=[BF["var"], BF["ceps"]], w=[BF["rstd"]])
            E("scalar", lambda x: x.activation(out=S["rstd"][:, :], in_=S["rstd"][:, :],
                                               func=AF.Exp, scale=-0.5),
              s=[BF["rstd"]], w=[BF["rstd"]])
            E("vector", lambda x: x.scalar_tensor_tensor(
                out=S["nmr"][:, :], in0=S["mean"][:, :], scalar=-1.0,
                in1=S["rstd"][:, :], op0=ALU.mult, op1=ALU.mult),
              r=[BF["mean"]], s=[BF["rstd"]], w=[BF["nmr"]])

        def ffn_body(zof, xsrc, xdst):
            for g in range(4):
                for k in range(2):
                    pA = "psA" if k == 0 else "psB"
                    w1T = "w1T0" if k == 0 else "w1T1"
                    b1r = "b1r0" if k == 0 else "b1r1"
                    E("tensor", lambda x, pA=pA, b1r=b1r: x.matmul(
                        out=S[pA][:, :], lhsT=S[b1r][:, :], rhs=S["onesb"][:, :],
                        start=True, stop=False),
                      r=[BF[b1r], BF["onesb"]], w=[BF[pA]])
                    for jj in range(4):
                        j = 4 * g + jj
                        E("tensor", lambda x, pA=pA, w1T=w1T, j=j, jj=jj: x.matmul(
                            out=S[pA][:, jj * 128:(jj + 1) * 128],
                            lhsT=S[w1T][:, :],
                            rhs=S["z1T"][:, j * 128:(j + 1) * 128],
                            start=False, stop=(jj == 3)),
                          r=[BF[w1T], BF["z1T"]], w=[BF[pA]])
                    hb = "hb0" if k == 0 else "hb1"
                    E("scalar", lambda x, pA=pA, hb=hb: x.activation(
                        out=S[hb][:, :], in_=S[pA][:, :], func=AF.Gelu),
                      r=[BF[pA]], w=[BF[hb]])
                E("tensor", lambda x: x.matmul(
                    out=S["psA"][:, :], lhsT=S["onesb"][:, 0:128],
                    rhs=S["b2r4"][:, :], start=True, stop=False),
                  r=[BF["onesb"], BF["b2r4"]], w=[BF["psA"]])
                for k in range(2):
                    hb = "hb0" if k == 0 else "hb1"
                    w2T = "w2T0" if k == 0 else "w2T1"
                    for jj in range(4):
                        E("tensor", lambda x, hb=hb, w2T=w2T, jj=jj, k=k: x.matmul(
                            out=S["psA"][:, jj * 128:(jj + 1) * 128],
                            lhsT=S[hb][:, jj * 128:(jj + 1) * 128],
                            rhs=S[w2T][:, :],
                            start=False, stop=(k == 1 and jj == 3)),
                          r=[BF[hb], BF[w2T]], w=[BF["psA"]])
                E("vector", lambda x, g=g, zof=zof: x.tensor_add(
                    out=S["zob"][:, g * 512:(g + 1) * 512],
                    in0=S["psA"][:, :], in1=S[zof][:, g * 512:(g + 1) * 512]),
                  r=[BF["psA"], BF[zof]], w=[BF["zob"]])
            transpose16("zob", "zoT")
            for fc in range(4):
                pw = "psW0" if fc % 2 == 0 else "psW1"
                E("tensor", lambda x, fc=fc, pw=pw: x.matmul(
                    out=S[pw][:, :], lhsT=S["onesb"][:, 0:128],
                    rhs=S["b3p"][:, fc * 512:(fc + 1) * 512],
                    start=True, stop=False),
                  r=[BF["onesb"], BF["b3p"]], w=[BF[pw]])
                for j in range(ND):
                    E("tensor", lambda x, fc=fc, j=j, pw=pw: x.matmul(
                        out=S[pw][:, :],
                        lhsT=S["zoT"][:, j * 128:(j + 1) * 128],
                        rhs=S["w3T"][:, j * D + fc * 512: j * D + (fc + 1) * 512],
                        start=False, stop=(j == ND - 1)),
                      r=[BF["zoT"], BF["w3T"]], w=[BF[pw]])
                E("vector", lambda x, fc=fc, pw=pw, xsrc=xsrc, xdst=xdst: x.tensor_add(
                    out=S[xdst][:, fc * 512:(fc + 1) * 512],
                    in0=S[pw][:, :], in1=S[xsrc][:, fc * 512:(fc + 1) * 512]),
                  r=[BF[pw], BF[xsrc]], w=[BF[xdst]])

        def router(src_rows_fn, srcbuf, l_in_t, l_in_buf, l_out_t, l_out_buf):
            for b in range(4):
                for i in range(2):
                    ld(S["rxt"][:, :], src_rows_fn(b, i), [srcbuf], [BF["rxt"]])
                    ln_stats("rxt")
                    E("scalar", lambda x: x.activation(
                        out=S["zscr"][:, :], in_=S["rxt"][:, :], func=AF.Identity,
                        bias=S["nmr"][:, 0:1], scale=S["rstd"][:, 0:1]),
                      r=[BF["rxt"]], s=[BF["nmr"], BF["rstd"]], w=[BF["zscr"]])
                    for ee in range(8):
                        swsl = "rsw" if ee % 2 == 0 else "rsw2"
                        src = t_sw.ap()[ee:ee + 1,
                                        i * 262144:(i + 1) * 262144]
                        src = src.rearrange("a (p f) -> (a p) f", p=128)
                        ld(S[swsl][:, :], src, [dbuf["sw"]], [BF[swsl]])
                        E("vector", lambda x, swsl=swsl: x.tensor_mul(
                            out=S["rprod"][:, :], in0=S["zscr"][:, :],
                            in1=S[swsl][:, :]),
                          r=[BF["zscr"], BF[swsl]], w=[BF["rprod"]])
                        if i == 0:
                            E("vector", lambda x, ee=ee: x.reduce_sum(
                                out=S["racc"][:, ee:ee + 1], in_=S["rprod"][:, :],
                                axis=AX.X),
                              r=[BF["rprod"]], w=[BF["racc"]])
                        else:
                            E("vector", lambda x: x.reduce_sum(
                                out=S["rpart"][:, :], in_=S["rprod"][:, :],
                                axis=AX.X),
                              r=[BF["rprod"]], w=[BF["rpart"]])
                            E("vector", lambda x, ee=ee: x.tensor_add(
                                out=S["racc2"][:, ee:ee + 1],
                                in0=S["racc"][:, ee:ee + 1], in1=S["rpart"][:, :]),
                              r=[BF["racc"], BF["rpart"]], w=[BF["racc2"]])
                E("tensor", lambda x, b=b: x.matmul(
                    out=S["psQ"][0:8, 256 + b:256 + b + 1], lhsT=S["racc2"][:, :],
                    rhs=S["onesc"][:, :], start=True, stop=True),
                  r=[BF["racc2"], BF["onesc"]], w=[BF["psQ"]])
            E("vector", lambda x: x.tensor_copy(out=S["sgrow"][:, :],
                                                in_=S["psQ"][0:8, 256:260]),
              r=[BF["psQ"]], w=[BF["sgrow"]])
            E("gpsimd", lambda x, t=l_in_t: x.dma_start(
                out=t.ap().rearrange("b e -> e b"), in_=S["sgrow"][:, :],
                allow_slow_non_contiguous=True),
              r=[BF["sgrow"]], w=[l_in_buf], dma=True)
            E("gpsimd", lambda x, ti=l_in_t, to=l_out_t: x.collective_compute(
                "AllReduce", ALU.add, replica_groups=[list(range(8))],
                ins=[ti.ap()], outs=[to.ap()]),
              r=[l_in_buf], w=[l_out_buf], cc=True)
            ld(S["lg"][:, :], l_out_t.ap(), [l_out_buf], [BF["lg"]])
            bcast(S["swbb"][:, :], S["swb"][0:1, :], 4, [BF["swb"]], [BF["swbb"]])
            E("vector", lambda x: x.tensor_add(out=S["lg"][:, :], in0=S["lg"][:, :],
                                               in1=S["swbb"][:, :]),
              r=[BF["lg"], BF["swbb"]], w=[BF["lg"]])

        def selection():
            E("tensor", lambda x: x.matmul(out=S["psQ"][0:1, 0:8],
                                           lhsT=S["bsel"][:, :], rhs=S["lg"][:, :],
                                           start=True, stop=True),
              r=[BF["bsel"], BF["lg"]], w=[BF["psQ"]])
            E("vector", lambda x: x.tensor_copy(out=S["lrow"][:, :],
                                                in_=S["psQ"][0:1, 0:8]),
              r=[BF["psQ"]], w=[BF["lrow"]])
            E("vector", lambda x: x.reduce_max(out=S["nmx"][:, :],
                                               in_=S["lrow"][:, :], axis=AX.X),
              r=[BF["lrow"]], w=[BF["nmx"]])
            E("vector", lambda x: x.tensor_scalar_mul(out=S["nmx"][:, :],
                                                      in0=S["nmx"][:, :],
                                                      scalar1=-1.0),
              r=[BF["nmx"]], w=[BF["nmx"]])
            E("vector", lambda x: x.tensor_scalar_add(out=S["j8"][:, :],
                                                      in0=S["lrow"][:, :],
                                                      scalar1=S["nmx"][0:1, 0:1]),
              r=[BF["lrow"]], s=[BF["nmx"]], w=[BF["j8"]])
            E("vector", lambda x: x.tensor_scalar_max(out=S["j8"][:, :],
                                                      in0=S["j8"][:, :],
                                                      scalar1=-80.0),
              r=[BF["j8"]], w=[BF["j8"]])
            E("scalar", lambda x: x.activation(out=S["probs"][:, :],
                                               in_=S["j8"][:, :], func=AF.Exp),
              r=[BF["j8"]], w=[BF["probs"]])
            E("vector", lambda x: x.reduce_sum(out=S["ssum"][:, :],
                                               in_=S["probs"][:, :], axis=AX.X),
              r=[BF["probs"]], w=[BF["ssum"]])
            E("vector", lambda x: x.reciprocal(out=S["rsum"][:, :],
                                               in_=S["ssum"][:, :]),
              r=[BF["ssum"]], w=[BF["rsum"]])
            E("vector", lambda x: x.tensor_scalar_mul(out=S["probs"][:, :],
                                                      in0=S["probs"][:, :],
                                                      scalar1=S["rsum"][0:1, 0:1]),
              r=[BF["probs"]], s=[BF["rsum"]], w=[BF["probs"]])
            E("vector", lambda x: x.reduce_max(out=S["mx1"][:, :],
                                               in_=S["probs"][:, :], axis=AX.X),
              r=[BF["probs"]], w=[BF["mx1"]])
            E("vector", lambda x: x.tensor_single_scalar(
                out=S["m1"][:, :], in_=S["probs"][:, :],
                scalar=S["mx1"][0:1, 0:1], op=ALU.is_ge),
              r=[BF["probs"]], s=[BF["mx1"]], w=[BF["m1"]])
            E("vector", lambda x: x.scalar_tensor_tensor(
                out=S["p2"][:, :], in0=S["m1"][:, :], scalar=-1000.0,
                in1=S["probs"][:, :], op0=ALU.mult, op1=ALU.add),
              r=[BF["m1"], BF["probs"]], w=[BF["p2"]])
            E("vector", lambda x: x.reduce_max(out=S["mx2"][:, :],
                                               in_=S["p2"][:, :], axis=AX.X),
              r=[BF["p2"]], w=[BF["mx2"]])
            E("vector", lambda x: x.tensor_single_scalar(
                out=S["m2"][:, :], in_=S["p2"][:, :],
                scalar=S["mx2"][0:1, 0:1], op=ALU.is_ge),
              r=[BF["p2"]], s=[BF["mx2"]], w=[BF["m2"]])
            E("vector", lambda x: x.tensor_mul(out=S["j8"][:, :],
                                               in0=S["probs"][:, :],
                                               in1=S["m1"][:, :]),
              r=[BF["probs"], BF["m1"]], w=[BF["j8"]])
            E("vector", lambda x: x.reduce_sum(out=S["g1"][:, :], in_=S["j8"][:, :],
                                               axis=AX.X),
              r=[BF["j8"]], w=[BF["g1"]])
            E("vector", lambda x: x.tensor_mul(out=S["j8"][:, :],
                                               in0=S["probs"][:, :],
                                               in1=S["m2"][:, :]),
              r=[BF["probs"], BF["m2"]], w=[BF["j8"]])
            E("vector", lambda x: x.reduce_sum(out=S["g2"][:, :], in_=S["j8"][:, :],
                                               axis=AX.X),
              r=[BF["j8"]], w=[BF["g2"]])
            E("vector", lambda x: x.tensor_scalar_mul(out=S["coef"][:, :],
                                                      in0=S["m1"][:, :],
                                                      scalar1=S["g1"][0:1, 0:1]),
              r=[BF["m1"]], s=[BF["g1"]], w=[BF["coef"]])
            E("vector", lambda x: x.tensor_scalar_mul(out=S["coefb"][:, :],
                                                      in0=S["m2"][:, :],
                                                      scalar1=S["g2"][0:1, 0:1]),
              r=[BF["m2"]], s=[BF["g2"]], w=[BF["coefb"]])
            E("vector", lambda x: x.tensor_add(out=S["coef"][:, :],
                                               in0=S["coef"][:, :],
                                               in1=S["coefb"][:, :]),
              r=[BF["coef"], BF["coefb"]], w=[BF["coef"]])
            for nm, dst in (("m1", "m1T"), ("m2", "m2T"), ("coef", "coefT")):
                E("gpsimd", lambda x, nm=nm: x.dma_start(
                    out=t_tiny.ap()[0:1, 0:8], in_=S[nm][:, :]),
                  r=[BF[nm]], w=[dbuf["tiny"]], dma=True)
                ld(S[dst][:, :],
                   t_tiny.ap()[0:1, 0:8].rearrange("a (p f) -> (a p) f", p=8),
                   [dbuf["tiny"]], [BF[dst]], slow=True)
            for k, (mT, b1r) in enumerate((("m1T", "b1r0"), ("m2T", "b1r1"))):
                E("tensor", lambda x, mT=mT, k=k: x.matmul(
                    out=S["psQ"][0:1, 128 * k:128 * k + 128], lhsT=S[mT][:, :],
                    rhs=S["b1s"][:, :], start=True, stop=True),
                  r=[BF[mT], BF["b1s"]], w=[BF["psQ"]])
                E("scalar", lambda x, b1r=b1r, k=k: x.activation(
                    out=S[b1r][:, :], in_=S["psQ"][0:1, 128 * k:128 * k + 128],
                    func=AF.Copy),
                  r=[BF["psQ"]], w=[BF[b1r]])
            E("tensor", lambda x: x.matmul(
                out=S["psQ"][0:1, 256:384], lhsT=S["coefT"][:, :],
                rhs=S["b2s"][:, :], start=True, stop=True),
              r=[BF["coefT"], BF["b2s"]], w=[BF["psQ"]])
            E("scalar", lambda x: x.activation(
                out=S["b2r4"][:, 0:128], in_=S["psQ"][0:1, 256:384], func=AF.Copy),
              r=[BF["psQ"]], w=[BF["b2r4"]])
            b2v = S["b2r4"][0:1, 0:128]
            E("gpsimd", lambda x: x.dma_start(
                out=S["b2r4"][0:1, 128:512],
                in_=AP(b2v.tensor, b2v.offset, [[1, 1], [0, 3], [1, 128]])),
              r=[BF["b2r4"]], w=[BF["b2r4"]], dma=True)
            for (mname, wall, wT, scl) in (
                    ("m1", "w1all", "w1T0", None), ("m2", "w1all", "w1T1", None),
                    ("m1", "w2all", "w2T0", "g1"), ("m2", "w2all", "w2T1", "g2")):
                if scl is None:
                    bcast(S["mrow"][:, :], S[mname][0:1, :], 128,
                          [BF[mname]], [BF["mrow"]])
                else:
                    E("vector", lambda x, mname=mname, scl=scl: x.tensor_scalar_mul(
                        out=S["j8"][:, :], in0=S[mname][:, :],
                        scalar1=S[scl][0:1, 0:1]),
                      r=[BF[mname]], s=[BF[scl]], w=[BF["j8"]])
                    bcast(S["mrow"][:, :], S["j8"][0:1, :], 128,
                          [BF["j8"]], [BF["mrow"]])
                E("vector", lambda x, wall=wall: x.tensor_scalar_mul(
                    out=S["ws0"][:, :], in0=S[wall][:, 0:128],
                    scalar1=S["mrow"][:, 0:1]),
                  r=[BF[wall]], s=[BF["mrow"]], w=[BF["ws0"]])
                for ee in range(1, 8):
                    src_b = "ws0" if ee % 2 == 1 else "ws1"
                    dst_b = "ws1" if ee % 2 == 1 else "ws0"
                    E("vector", lambda x, wall=wall, ee=ee, src_b=src_b, dst_b=dst_b:
                      x.scalar_tensor_tensor(
                        out=S[dst_b][:, :], in0=S[wall][:, ee * 128:(ee + 1) * 128],
                        scalar=S["mrow"][:, ee:ee + 1], in1=S[src_b][:, :],
                        op0=ALU.mult, op1=ALU.add),
                      r=[BF[wall], BF[src_b]], s=[BF["mrow"]], w=[BF[dst_b]])
                E("scalar", lambda x: x.activation(out=S["wsb"][:, :],
                                                   in_=S["ws1"][:, :], func=AF.Copy),
                  r=[BF["ws1"]], w=[BF["wsb"]])
                E("tensor", lambda x: x.transpose(
                    out=S["psT"][:, 0:128], in_=S["wsb"][:, :],
                    identity=S["ident"][:, :]),
                  r=[BF["wsb"], BF["ident"]], w=[BF["psT"]])
                E("vector", lambda x, wT=wT: x.tensor_copy(out=S[wT][:, :],
                                                           in_=S["psT"][:, 0:128]),
                  r=[BF["psT"]], w=[BF[wT]])

        def rx_rows(b, i):
            return t_rx.ap()[b * 256 + i * 128: b * 256 + (i + 1) * 128, :]
        router(rx_rows, dbuf["rx"], t_l1i, dbuf["l1i"], t_l1o, dbuf["l1o"])
        selection()
        dbg("lg", S["lg"][:, :], [BF["lg"]])
        dbg("racc2", S["racc2"][:, :], [BF["racc2"]])
        dbg("sgrow", S["sgrow"][:, :], [BF["sgrow"]])
        dbg("lrow", S["lrow"][:, :], [BF["lrow"]])
        dbg("probs", S["probs"][:, :], [BF["probs"]])
        dbg("m1", S["m1"][:, :], [BF["m1"]])
        dbg("m2", S["m2"][:, :], [BF["m2"]])
        dbg("g1", S["g1"][:, :], [BF["g1"]])
        dbg("g2", S["g2"][:, :], [BF["g2"]])
        dbg("ws1", S["ws1"][:, :], [BF["ws1"]])
        dbg("ipb", S["ipb"][:, :], [BF["ipb"]])
        dbg("App", S["App"][:, :], [BF["App"]])
        dbg("Arep", S["Arep"][:, 0:256], [BF["Arep"]])

        E("gpsimd", lambda x: x.memset(S["state"][:, :], 0.0), w=[BF["state"]])

        def mk_z1T(srcname):
            transpose16(srcname, "z1T")

        def pass1_chunk(c, halo):
            row0 = 4096 if halo else c * TC
            ld(S["xbuf"][:, :], t_xloc.ap()[row0:row0 + TC, :],
               [dbuf["xloc"]], [BF["xbuf"]])
            ln_stats("xbuf")
            E("scalar", lambda x: x.activation(out=S["z1b"][:, :], in_=S["xbuf"][:, :],
                                               func=AF.Identity, bias=S["nmr"][:, 0:1],
                                               scale=S["rstd"][:, 0:1]),
              r=[BF["xbuf"]], s=[BF["nmr"], BF["rstd"]], w=[BF["z1b"]])
            mk_z1T("z1b")
            ffn_body("z1b", "xbuf", "x1b")
            ln_stats("x1b")
            E("scalar", lambda x: x.activation(out=S["rstd2"][:, :],
                                               in_=S["rstd"][:, :], func=AF.Square),
              s=[BF["rstd"]], w=[BF["rstd2"]])
            E("vector", lambda x: x.tensor_mul(out=S["rstd2"][:, :],
                                               in0=S["var"][:, :],
                                               in1=S["rstd2"][:, :]),
              r=[BF["var"], BF["rstd2"]], w=[BF["rstd2"]])
            E("scalar", lambda x: x.activation(out=S["rstd2"][:, :],
                                               in_=S["rstd2"][:, :],
                                               func=AF.Ln, bias=S["ceps"][:, 0:1]),
              s=[BF["rstd2"], BF["ceps"]], w=[BF["rstd2"]])
            E("scalar", lambda x: x.activation(out=S["rstd2"][:, :],
                                               in_=S["rstd2"][:, :],
                                               func=AF.Exp, scale=-0.5),
              s=[BF["rstd2"]], w=[BF["rstd2"]])
            E("vector", lambda x: x.tensor_mul(out=S["scomb"][:, :],
                                               in0=S["rstd"][:, :],
                                               in1=S["rstd2"][:, :]),
              r=[BF["rstd"], BF["rstd2"]], w=[BF["scomb"]])
            E("vector", lambda x: x.scalar_tensor_tensor(
                out=S["nms"][:, :], in0=S["mean"][:, :], scalar=-1.0,
                in1=S["scomb"][:, :], op0=ALU.mult, op1=ALU.mult),
              r=[BF["mean"]], s=[BF["scomb"]], w=[BF["nms"]])
            if not halo:
                E("scalar", lambda x: x.activation(
                    out=S["zscr"][:, :], in_=S["x1b"][:, :], func=AF.Identity,
                    bias=S["nmr"][:, 0:1], scale=S["rstd"][:, 0:1]),
                  r=[BF["x1b"]], s=[BF["nmr"], BF["rstd"]], w=[BF["zscr"]])
                E("vector", lambda x: x.tensor_add(out=S["xbuf"][:, :],
                                                   in0=S["x1b"][:, :],
                                                   in1=S["zscr"][:, :]),
                  r=[BF["x1b"], BF["zscr"]], w=[BF["xbuf"]])
                E("gpsimd", lambda x, c=c: x.dma_start(
                    out=t_x2p.ap()[c * TC:(c + 1) * TC, :], in_=S["xbuf"][:, :]),
                  r=[BF["xbuf"]], w=[dbuf["x2p"]], dma=True)
            E("scalar", lambda x: x.activation(out=S["bft"][:, :], in_=S["x1b"][:, :],
                                               func=AF.Identity, bias=S["nms"][:, 0:1],
                                               scale=S["scomb"][:, 0:1]),
              r=[BF["x1b"]], s=[BF["nms"], BF["scomb"]], w=[BF["bft"]])
            mk_z1T("bft")
            for j in range(ND):
                E("tensor", lambda x, j=j: x.matmul(
                    out=S["psS"][0:DIN, 0:TC],
                    lhsT=S["ipWT"][:, j * DIN:(j + 1) * DIN],
                    rhs=S["z1T"][:, j * 128:(j + 1) * 128],
                    start=(j == 0), stop=(j == ND - 1)),
                  r=[BF["ipWT"], BF["z1T"]], w=[BF["psS"]])
            slot = "cvxH" if halo else ("cvx%d" % (c % 3))
            E("vector", lambda x, slot=slot: x.tensor_scalar(
                out=S[slot][:, 1:129], in0=S["psS"][0:DIN, 0:TC],
                scalar1=S["ipb"][:, 0:1], scalar2=None, op0=ALU.add),
              r=[BF["psS"]], s=[BF["ipb"]], w=[BF[slot]])

        def scan_block(c, main):
            stname = "state" if main else "xinit"
            for n in range(NST):
                E("sync", lambda x, n=n: x.dma_start(
                    out=S["sc1"][:, n * TC:(n + 1) * TC], in_=S["spb"][:, :]),
                  r=[BF["spb"]], w=[BF["sc1"]], dma=True)
            E("vector", lambda x: x.tensor_mul(out=S["sc1"][:, :], in0=S["sc1"][:, :],
                                               in1=S["Arep"][:, :]),
              r=[BF["sc1"], BF["Arep"]], w=[BF["sc1"]])
            E("scalar", lambda x: x.activation(out=S["sc2"][:, :], in_=S["sc1"][:, :],
                                               func=AF.Exp),
              r=[BF["sc1"]], w=[BF["sc2"]])
            if main:
                bsrc = S["dbc"][32:32 + NST, :]
                E("gpsimd", lambda x, bsrc=bsrc: x.dma_start(
                    out=S["sc3"][0:1, :], in_=bsrc),
                  r=[BF["dbc"]], w=[BF["sc3"]], dma=True)
                bcast(S["sc1"][:, :], S["sc3"][0:1, :], DIN,
                      [BF["sc3"]], [BF["sc1"]])
                for n in range(NST):
                    E("sync", lambda x, n=n: x.dma_start(
                        out=S["sc3"][:, n * TC:(n + 1) * TC], in_=S["Gb"][:, :]),
                      r=[BF["Gb"]], w=[BF["sc3"]], dma=True)
                E("vector", lambda x: x.tensor_mul(out=S["sc3"][:, :],
                                                   in0=S["sc3"][:, :],
                                                   in1=S["sc1"][:, :]),
                  r=[BF["sc3"], BF["sc1"]], w=[BF["sc3"]])
            v2 = S["sc2"][:, :].rearrange("p (n t) -> p n t", t=TC)
            v3 = S["sc3"][:, :].rearrange("p (n t) -> p n t", t=TC)
            E("vector", lambda x, v2=v2, stname=stname: x.tensor_mul(
                out=S["t16"][:, :], in0=v2[:, :, 0], in1=S[stname][:, :]),
              r=[BF["sc2"], BF[stname]], w=[BF["t16"]])
            E("vector", lambda x, v2=v2: x.memset(v2[:, :, 0], 0.0),
              r=[BF["sc2"]], w=[BF["sc2"]])
            if main:
                E("vector", lambda x, v3=v3: x.tensor_add(
                    out=v3[:, :, 0], in0=v3[:, :, 0], in1=S["t16"][:, :]),
                  r=[BF["sc3"], BF["t16"]], w=[BF["sc3"]])
            else:
                E("vector", lambda x, v3=v3: x.tensor_copy(
                    out=v3[:, :, 0], in_=S["t16"][:, :]),
                  r=[BF["t16"]], w=[BF["sc3"]])
            E("vector", lambda x: x.tensor_tensor_scan(
                out=S["sc1"][:, :], data0=S["sc2"][:, :], data1=S["sc3"][:, :],
                initial=0.0, op0=ALU.mult, op1=ALU.add),
              r=[BF["sc2"], BF["sc3"]], w=[BF["sc1"]])
            v1 = S["sc1"][:, :].rearrange("p (n t) -> p n t", t=TC)
            E("vector", lambda x, v1=v1, stname=stname: x.tensor_copy(
                out=S[stname][:, :], in_=v1[:, :, TC - 1]),
              r=[BF["sc1"]], w=[BF[stname]])
            if main:
                csrc = S["dbc"][64:64 + NST, :]
                E("gpsimd", lambda x, csrc=csrc: x.dma_start(
                    out=S["sc3"][0:1, :], in_=csrc),
                  r=[BF["dbc"]], w=[BF["sc3"]], dma=True)
                bcast(S["sc2"][:, :], S["sc3"][0:1, :], DIN,
                      [BF["sc3"]], [BF["sc2"]])
            else:
                E("gpsimd", lambda x: x.dma_start(
                    out=S["zscr"][0:1, 0:NST * TC], in_=S["Cc"][:, :]),
                  r=[BF["Cc"]], w=[BF["zscr"]], dma=True)
                bcast(S["sc2"][:, :], S["zscr"][0:1, 0:NST * TC], DIN,
                      [BF["zscr"]], [BF["sc2"]])
            E("vector", lambda x: x.tensor_mul(out=S["sc1"][:, :], in0=S["sc1"][:, :],
                                               in1=S["sc2"][:, :]),
              r=[BF["sc1"], BF["sc2"]], w=[BF["sc1"]])
            nv = S["sc1"][:, :].rearrange("p (n t) -> p t n", t=TC)
            E("vector", lambda x, nv=nv: x.tensor_reduce(out=S["ybuf"][:, :], in_=nv,
                                                         axis=AX.X, op=ALU.add),
              r=[BF["sc1"]], w=[BF["ybuf"]])
            if main:
                E("vector", lambda x, c=c: x.scalar_tensor_tensor(
                    out=S["ystore"][:, c * TC:(c + 1) * TC], in0=S["ubuf"][:, :],
                    scalar=S["Dp"][:, 0:1], in1=S["ybuf"][:, :],
                    op0=ALU.mult, op1=ALU.add),
                  r=[BF["ubuf"], BF["ybuf"]], s=[BF["Dp"]], w=[BF["ystore"]])
            else:
                E("vector", lambda x, c=c: x.tensor_add(
                    out=S["ystore"][:, c * TC:(c + 1) * TC],
                    in0=S["ystore"][:, c * TC:(c + 1) * TC], in1=S["ybuf"][:, :]),
                  r=[BF["ystore"], BF["ybuf"]], w=[BF["ystore"]])

        def conv_scan(c):
            slot = "cvx%d" % (c % 3)
            if c > 0:
                prev = "cvx%d" % ((c - 1) % 3)
                E("vector", lambda x, slot=slot, prev=prev: x.tensor_copy(
                    out=S[slot][:, 0:1], in_=S[prev][:, 128:129]),
                  r=[BF[prev]], w=[BF[slot]])
            else:
                E("vector", lambda x, slot=slot: x.tensor_scalar_mul(
                    out=S[slot][:, 0:1], in0=S["cvxH"][:, 128:129],
                    scalar1=S["ml"][:, 0:1]),
                  r=[BF["cvxH"]], s=[BF["ml"]], w=[BF[slot]])
            if c < NCH - 1:
                nxt = "cvx%d" % ((c + 1) % 3)
                E("vector", lambda x, slot=slot, nxt=nxt: x.tensor_copy(
                    out=S[slot][:, 129:130], in_=S[nxt][:, 1:2]),
                  r=[BF[nxt]], w=[BF[slot]])
            else:
                E("vector", lambda x, slot=slot: x.tensor_scalar_mul(
                    out=S[slot][:, 129:130], in0=S["cvxH"][:, 1:2],
                    scalar1=S["mr"][:, 0:1]),
                  r=[BF["cvxH"]], s=[BF["mr"]], w=[BF[slot]])
            E("vector", lambda x, slot=slot: x.tensor_scalar_mul(
                out=S["cvA"][:, :], in0=S[slot][:, 0:TC],
                scalar1=S["convw"][:, 0:1]),
              r=[BF[slot]], s=[BF["convw"]], w=[BF["cvA"]])
            E("vector", lambda x, slot=slot: x.scalar_tensor_tensor(
                out=S["cvB"][:, :], in0=S[slot][:, 1:TC + 1],
                scalar=S["convw"][:, 1:2], in1=S["cvA"][:, :],
                op0=ALU.mult, op1=ALU.add),
              r=[BF[slot], BF["cvA"]], s=[BF["convw"]], w=[BF["cvB"]])
            E("vector", lambda x, slot=slot: x.scalar_tensor_tensor(
                out=S["cvA"][:, :], in0=S[slot][:, 2:TC + 2],
                scalar=S["convw"][:, 2:3], in1=S["cvB"][:, :],
                op0=ALU.mult, op1=ALU.add),
              r=[BF[slot], BF["cvB"]], s=[BF["convw"]], w=[BF["cvA"]])
            E("scalar", lambda x: x.activation(out=S["ubuf"][:, :], in_=S["cvA"][:, :],
                                               func=AF.Silu, bias=S["convb"][:, 0:1]),
              r=[BF["cvA"]], s=[BF["convb"]], w=[BF["ubuf"]])
            E("scalar", lambda x: x.activation(out=S["ub16"][:, :], in_=S["ubuf"][:, :],
                                               func=AF.Copy),
              r=[BF["ubuf"]], w=[BF["ub16"]])
            E("tensor", lambda x: x.matmul(out=S["psQ"][0:DIN, 0:TC],
                                           lhsT=S["xpWT"][:, :], rhs=S["ub16"][:, :],
                                           start=True, stop=True),
              r=[BF["xpWT"], BF["ub16"]], w=[BF["psQ"]])
            E("vector", lambda x: x.tensor_copy(out=S["dbc"][:, :],
                                                in_=S["psQ"][0:DIN, 0:TC]),
              r=[BF["psQ"]], w=[BF["dbc"]])
            E("scalar", lambda x: x.activation(out=S["dr6"][:, :],
                                               in_=S["dbc"][0:DTR, :], func=AF.Copy),
              r=[BF["dbc"]], w=[BF["dr6"]])
            E("tensor", lambda x: x.matmul(out=S["psQ"][0:DIN, 128:128 + TC],
                                           lhsT=S["dtWT"][:, :], rhs=S["dr6"][:, :],
                                           start=True, stop=True),
              r=[BF["dtWT"], BF["dr6"]], w=[BF["psQ"]])
            E("scalar", lambda x: x.activation(out=S["sca"][:, :],
                                               in_=S["psQ"][0:DIN, 128:128 + TC],
                                               func=AF.Relu, bias=S["dtb"][:, 0:1]),
              r=[BF["psQ"]], s=[BF["dtb"]], w=[BF["sca"]])
            E("scalar", lambda x: x.activation(out=S["scb"][:, :],
                                               in_=S["psQ"][0:DIN, 128:128 + TC],
                                               func=AF.Abs, bias=S["dtb"][:, 0:1]),
              r=[BF["psQ"]], s=[BF["dtb"]], w=[BF["scb"]])
            E("vector", lambda x: x.tensor_scalar_min(out=S["scb"][:, :],
                                                      in0=S["scb"][:, :],
                                                      scalar1=80.0),
              r=[BF["scb"]], w=[BF["scb"]])
            E("scalar", lambda x: x.activation(out=S["scb"][:, :], in_=S["scb"][:, :],
                                               func=AF.Exp, scale=-1.0),
              s=[BF["scb"]], w=[BF["scb"]])
            E("scalar", lambda x: x.activation(out=S["scb"][:, :], in_=S["scb"][:, :],
                                               func=AF.Ln, bias=S["cone"][0:DIN, 0:1]),
              s=[BF["scb"], BF["cone"]], w=[BF["scb"]])
            E("vector", lambda x: x.tensor_add(out=S["spb"][:, :], in0=S["sca"][:, :],
                                               in1=S["scb"][:, :]),
              r=[BF["sca"], BF["scb"]], w=[BF["spb"]])
            E("vector", lambda x: x.tensor_mul(out=S["Gb"][:, :], in0=S["spb"][:, :],
                                               in1=S["ubuf"][:, :]),
              r=[BF["spb"], BF["ubuf"]], w=[BF["Gb"]])
            E("gpsimd", lambda x, c=c: x.dma_start(
                out=t_dsc.ap()[:, c * TC:(c + 1) * TC], in_=S["spb"][:, :]),
              r=[BF["spb"]], w=[dbuf["dsc"]], dma=True)
            E("gpsimd", lambda x, c=c: x.dma_start(
                out=t_csc.ap()[:, c * TC:(c + 1) * TC], in_=S["dbc"][64:64 + NST, :]),
              r=[BF["dbc"]], w=[dbuf["csc"]], dma=True)
            scan_block(c, True)

        pass1_chunk(0, halo=True)
        for c in range(NCH):
            pass1_chunk(c, halo=False)
            if c == 0:
                dbg("z1f0", S["z1f"][:, 0:256], [BF["z1f"]])
                dbg("x1b0", S["x1b"][:, 0:256], [BF["x1b"]])
                dbg("xbuf0", S["xbuf"][:, 0:256], [BF["xbuf"]])
            if c >= 1:
                conv_scan(c - 1)
            if c == 1:
                dbg("ip0", S["cvx0"][:, 0:130], [BF["cvx0"]])
                dbg("u0", S["ubuf"][:, 0:128], [BF["ubuf"]])
                dbg("sp0", S["spb"][:, 0:128], [BF["spb"]])
                dbg("dbc0", S["dbc"][:, 0:128], [BF["dbc"]])
                dbg("y0", S["ystore"][:, 0:128], [BF["ystore"]])
        conv_scan(NCH - 1)

        E("gpsimd", lambda x: x.dma_start(
            out=t_sti.ap().rearrange("a (p n) -> (a p) n", p=DIN),
            in_=S["state"][:, :]),
          r=[BF["state"]], w=[dbuf["sti"]], dma=True)
        E("gpsimd", lambda x: x.collective_compute(
            "AllGather", ALU.bypass, replica_groups=[list(range(8))],
            ins=[t_sti.ap()], outs=[t_sto.ap()]),
          r=[dbuf["sti"]], w=[dbuf["sto"]], cc=True)
        ld(S["zscr"][0:8, 0:DIN * NST], t_sto.ap(), [dbuf["sto"]], [BF["zscr"]])
        for i in range(3):
            pw = ("psW0", "psW1", "psS")[i]
            E("tensor", lambda x, i=i, pw=pw: x.matmul(
                out=S[pw][0:1, 0:512], lhsT=S["psel"][:, :],
                rhs=S["zscr"][0:8, i * 512:(i + 1) * 512], start=True, stop=True),
              r=[BF["psel"], BF["zscr"]], w=[BF[pw]])
            E("vector", lambda x, i=i, pw=pw: x.tensor_copy(
                out=S["zscr"][32:33, i * 512:(i + 1) * 512], in_=S[pw][0:1, 0:512]),
              r=[BF[pw]], w=[BF["zscr"]])
        E("gpsimd", lambda x: x.dma_start(out=t_xirow.ap(),
                                          in_=S["zscr"][32:33, 0:DIN * NST]),
          r=[BF["zscr"]], w=[dbuf["xirow"]], dma=True)
        ld(S["xinit"][:, :],
           t_xirow.ap().rearrange("a (p n) -> (a p) n", p=DIN),
           [dbuf["xirow"]], [BF["xinit"]])

        E("gpsimd", lambda x: x.memset(S["sc3"][:, :], 0.0), w=[BF["sc3"]])
        for c in range(NCH):
            ld(S["spb"][:, :], t_dsc.ap()[:, c * TC:(c + 1) * TC],
               [dbuf["dsc"]], [BF["spb"]])
            ld(S["Cc"][:, :], t_csc.ap()[:, c * TC:(c + 1) * TC],
               [dbuf["csc"]], [BF["Cc"]])
            scan_block(c, False)

        def oproj_chunk(c, dst_dram, dst_buf):
            ld(S["xbuf"][:, :], t_x2p.ap()[c * TC:(c + 1) * TC, :],
               [dbuf["x2p"]], [BF["xbuf"]])
            E("scalar", lambda x, c=c: x.activation(
                out=S["ybf"][:, :], in_=S["ystore"][:, c * TC:(c + 1) * TC],
                func=AF.Copy),
              r=[BF["ystore"]], w=[BF["ybf"]])
            for fc in range(4):
                pw = "psW0" if fc % 2 == 0 else "psW1"
                E("tensor", lambda x, fc=fc, pw=pw: x.matmul(
                    out=S[pw][:, :], lhsT=S["ybf"][:, :],
                    rhs=S["opWT"][:, fc * 512:(fc + 1) * 512],
                    start=True, stop=True),
                  r=[BF["ybf"], BF["opWT"]], w=[BF[pw]])
                E("vector", lambda x, fc=fc, pw=pw: x.tensor_add(
                    out=S["x1b"][:, fc * 512:(fc + 1) * 512], in0=S[pw][:, :],
                    in1=S["xbuf"][:, fc * 512:(fc + 1) * 512]),
                  r=[BF[pw], BF["xbuf"]], w=[BF["x1b"]])
            if dst_dram is not None:
                E("gpsimd", lambda x, c=c, d=dst_dram: x.dma_start(
                    out=d.ap()[c * TC:(c + 1) * TC, :], in_=S["x1b"][:, :]),
                  r=[BF["x1b"]], w=[dst_buf], dma=True)

        for c in range(16):
            oproj_chunk(c, t_x2b, dbuf["x2b"])

        def _a2a(x):
            with allow_ici_all_to_all():
                return x.collective_compute(
                    "AllToAll", ALU.bypass, replica_groups=[list(range(8))],
                    ins=[t_x2b.ap()], outs=[t_a2a.ap()])
        E("gpsimd", _a2a, r=[dbuf["x2b"]], w=[dbuf["a2a"]], cc=True)

        def a2a_rows(b, i):
            r0 = (2 * b) * 256 + i * 128
            return t_a2a.ap()[r0:r0 + 128, :]
        router(a2a_rows, dbuf["a2a"], t_l3i, dbuf["l3i"], t_l3o, dbuf["l3o"])
        selection()

        def pass2_chunk(c):
            if c < 16:
                ld(S["x1b"][:, :], t_x2b.ap()[c * TC:(c + 1) * TC, :],
                   [dbuf["x2b"]], [BF["x1b"]])
            else:
                oproj_chunk(c, None, None)
            ln_stats("x1b")
            E("scalar", lambda x: x.activation(out=S["z1b"][:, :], in_=S["x1b"][:, :],
                                               func=AF.Identity, bias=S["nmr"][:, 0:1],
                                               scale=S["rstd"][:, 0:1]),
              r=[BF["x1b"]], s=[BF["nmr"], BF["rstd"]], w=[BF["z1b"]])
            mk_z1T("z1b")
            ffn_body("z1b", "x1b", "zscr")
            E("gpsimd", lambda x, c=c: x.dma_start(
                out=t_yout.ap()[c * TC:(c + 1) * TC, :], in_=S["zscr"][:, :]),
              r=[BF["zscr"]], w=[dbuf["yout"]], dma=True)

        for c in range(NCH):
            pass2_chunk(c)

        final_dout = kb.cnt["dout"]
        final_din = kb.cnt["din"]

        def emit(engobj, engname):
            emitted = {}
            for fn, waits, sem, d in kb.ops[engname]:
                for smn, v in waits:
                    if emitted.get(smn, 0) < v:
                        engobj.wait_ge(sems[smn], v)
                        emitted[smn] = v
                instr = fn(engobj)
                instr.then_inc(sems[sem], d)

        @block.sync
        def _(sync):
            emit(sync, "sync")

        @block.vector
        def _(vector):
            emit(vector, "vector")

        @block.scalar
        def _(scalar):
            emit(scalar, "scalar")

        @block.tensor
        def _(tensor):
            emit(tensor, "tensor")

        @block.gpsimd
        def _(gpsimd):
            emit(gpsimd, "gpsimd")
            gpsimd.wait_ge(sems["dout"], final_dout)
            gpsimd.wait_ge(sems["din"], final_din)

    return nc


_NC_CACHE = None


def kernel(**inputs):
    global _NC_CACHE
    f32 = np.float32
    x = np.ascontiguousarray(inputs["L0_attn_feat_patch"], f32)
    sw = np.ascontiguousarray(inputs["L0_ffn_switch_w"], f32)
    ident = np.eye(128).astype(ml_dtypes.bfloat16)
    if _NC_CACHE is None:
        _NC_CACHE = build_program()
    nc = _NC_CACHE

    in_maps = []
    for core in range(8):
        b, h = core // 2, core % 2
        xo = x[b, h * HALF:(h + 1) * HALF]
        if h == 0:
            halorows = x[b, HALF:HALF + TC]
        else:
            halorows = x[b, HALF - TC:HALF]
        xloc = np.concatenate([xo, halorows], axis=0)
        rx = np.ascontiguousarray(
            x[:, core * 256:(core + 1) * 256, :]).reshape(1024, D)
        swsl = np.ascontiguousarray(
            sw.reshape(8, 2048, D)[:, core * 256:(core + 1) * 256, :]
        ).reshape(8, 524288)
        bsel = np.zeros((4, 1), f32)
        bsel[b, 0] = 1.0
        psel = np.zeros((8, 1), f32)
        if h == 1:
            psel[2 * b, 0] = 1.0
        ml = np.full((DIN, 1), 1.0 if h == 1 else 0.0, f32)
        mr = np.full((DIN, 1), 1.0 if h == 0 else 0.0, f32)
        in_maps.append({
            "xloc": np.ascontiguousarray(xloc),
            "rx": rx,
            "swsl": swsl,
            "swb": np.ascontiguousarray(inputs["L0_ffn_switch_b"], f32).reshape(1, 8),
            "w1": np.ascontiguousarray(inputs["L0_ffn_w1"], f32),
            "b1": np.ascontiguousarray(inputs["L0_ffn_b1"], f32),
            "w2": np.ascontiguousarray(inputs["L0_ffn_w2"], f32),
            "b2": np.ascontiguousarray(inputs["L0_ffn_b2"], f32),
            "w3": np.ascontiguousarray(inputs["L0_ffn_w3"], f32),
            "b3": np.ascontiguousarray(inputs["L0_ffn_b3"], f32).reshape(1, D),
            "lnw": np.ascontiguousarray(inputs["L0_attn_ln_w"], f32).reshape(1, D),
            "lnb": np.ascontiguousarray(inputs["L0_attn_ln_b"], f32).reshape(1, D),
            "ipW": np.ascontiguousarray(inputs["L0_attn_in_proj_W"], f32),
            "convw": np.ascontiguousarray(
                inputs["L0_attn_conv_w"], f32).reshape(DIN, 3),
            "convb": np.ascontiguousarray(
                inputs["L0_attn_conv_b"], f32).reshape(DIN, 1),
            "xpW": np.ascontiguousarray(inputs["L0_attn_x_proj_W"], f32),
            "dtW": np.ascontiguousarray(inputs["L0_attn_dt_proj_W"], f32),
            "dtb": np.ascontiguousarray(
                inputs["L0_attn_dt_proj_b"], f32).reshape(DIN, 1),
            "Alog": np.ascontiguousarray(inputs["L0_attn_A_log"], f32),
            "Dp": np.ascontiguousarray(
                inputs["L0_attn_D_param"], f32).reshape(DIN, 1),
            "opW": np.ascontiguousarray(inputs["L0_attn_out_proj_W"], f32),
            "ident": ident,
            "bsel": bsel,
            "psel": psel,
            "ml": ml,
            "mr": mr,
        })

    with allow_ici_all_to_all():
        res = run_bass_kernel_spmd(nc, in_maps, core_ids=list(range(8)))

    out = np.empty((B_, L_, D), f32)
    for core in range(8):
        b, h = core // 2, core % 2
        out[b, h * HALF:(h + 1) * HALF] = res.results[core]["yout"]
    return out
